# version 06
# baseline (speedup 1.0000x reference)
import contextlib
import numpy as np
import ml_dtypes

import concourse.bass as bass
import concourse.mybir as mybir
from concourse.bass_utils import run_bass_kernel_spmd

F32 = mybir.dt.float32
BF16 = mybir.dt.bfloat16
AF = mybir.ActivationFunctionType
ALU = mybir.AluOpType
AXX = mybir.AxisListType.X

NCORES = 8
D = 4096
L = 8192
T = 1024
TE = T + 3
NCH = 8
C = 128
H = 64
P = 64
N = 128
G = 8
EPS = 1e-5

BF = ml_dtypes.bfloat16

SPL = [(0, 343), (343, 685), (685, 1027)]
SPL2 = [(3, 515), (515, 1027)]
DEBUG_Y = False

ENG_NAMES = ("pe", "ve", "ac", "po", "sp", "gp")


def gp_tc(eng, dst, srcap):
    return eng.tensor_copy(dst, srcap)


class Sem:
    def __init__(self, st, nc, name):
        self.h = st.enter_context(nc.semaphore(name))
        self.n = 0

    def inc(self, instr, k=1):
        instr.then_inc(self.h, k)
        self.n += k
        return self.n


class Ctx:

    def __init__(self):
        self.nc = bass.Bass("TRN2", target_bir_lowering=False)
        self.st = contextlib.ExitStack()
        self.stacks = [self.st]
        self.block_cm = None
        self.block = None
        self.s = {}

    def sem(self, name):
        if name not in self.s:
            self.s[name] = Sem(self.st, self.nc, "sx_" + name)
        return self.s[name]

    def push(self):
        self.stacks.append(contextlib.ExitStack())

    def pop(self):
        self.stacks.pop().close()

    def open_block(self):
        self.block_cm = self.nc.Block()
        self.block = self.block_cm.__enter__()
        for nm in ENG_NAMES:
            self.s[nm] = Sem(self.st, self.nc, "s_" + nm)

    def close_block(self):
        self.block_cm.__exit__(None, None, None)

    def sb(self, name, shape, dt):
        return self.stacks[-1].enter_context(self.nc.sbuf_tensor(name, shape, dt))

    def ps(self, name, shape, dt):
        return self.st.enter_context(self.nc.psum_tensor(name, shape, dt))

    def din(self, name, shape, dt):
        return self.nc.dram_tensor(name, shape, dt, kind="ExternalInput")

    def dout(self, name, shape, dt):
        return self.nc.dram_tensor(name, shape, dt, kind="ExternalOutput")

    def dtmp(self, name, shape, dt):
        return self.nc.dram_tensor(name, shape, dt, kind="Internal")

    def sec(self, eng, fn):
        deco = getattr(self.block, {"pe": "tensor", "ve": "vector", "ac": "scalar",
                                    "po": "gpsimd", "sp": "sync", "gp": "gpsimd"}[eng])
        deco(fn)

    def barrier(self):
        snap = {nm: self.s[nm].n for nm in ("pe", "ve", "ac", "po")}
        def mk(eng):
            def f(e):
                e.wait_ge(self.s[eng].h, self.s[eng].n)
                if eng == "pe":
                    self.s["pe"].inc(e.nop())
                elif eng == "ve":
                    self.s["ve"].inc(e.engine_nop())
                elif eng == "ac":
                    self.s["ac"].inc(e.nop())
                elif eng == "po":
                    self.s["po"].inc(e.engine_nop())
            return f
        for eng in ("pe", "ve", "ac", "po"):
            self.sec(eng, mk(eng))
        post = {nm: self.s[nm].n for nm in ("pe", "ve", "ac", "po")}
        allsems = [(sm.h, sm.n) for nm, sm in self.s.items()
                   if nm not in ("sp", "gp")]

        def mkw(eng):
            def f(e):
                for h, nvv in allsems:
                    e.wait_ge(h, nvv)
            return f
        for eng in ("pe", "ve", "ac", "po", "sp", "gp"):
            self.sec(eng, mkw(eng))


def np_consts():
    s = np.arange(128)
    return {
        "id_bf": np.eye(128, dtype=BF),
        "id_f32": np.eye(128, dtype=np.float32),
        "m_cum": (s[:, None] <= s[None, :]).astype(BF),
        "maskb": np.where(s[:, None] > s[None, :], np.float32(-1e9),
                          np.float32(0.0)).astype(np.float32),
        "ut01f": (s[:, None] <= s[None, :]).astype(np.float32),
    }


def emit_common_consts(cx):
    nc = cx.nc
    d_idb = cx.din("id_bf", [128, 128], BF16)
    d_idf = cx.din("id_f32", [128, 128], F32)
    d_mcum = cx.din("m_cum", [128, 128], BF16)
    d_maskb = cx.din("maskb", [128, 128], F32)
    d_ut01f = cx.din("ut01f", [128, 128], F32)
    t = {
        "id_bf": cx.sb("id_bf_s", [128, 128], BF16),
        "id_f32": cx.sb("id_f32_s", [128, 128], F32),
        "m_cum": cx.sb("m_cum_s", [128, 128], BF16),
        "maskb": cx.sb("maskb_s", [128, 128], F32),
        "ut01f": cx.sb("ut01f_s", [128, 128], F32),
        "cst": cx.sb("cst", [128, 6], F32),
        "ones_bf": cx.sb("ones_bf", [128, 1], BF16),
        "ones_bf_t": cx.sb("ones_bf_t", [128, 128], BF16),
        "ones_row": cx.sb("ones_row", [1, 128], F32),
    }

    cld = cx.sem("cld")

    def f_sp(sp):
        for dst, src in ((t["id_bf"], d_idb), (t["id_f32"], d_idf),
                         (t["m_cum"], d_mcum), (t["maskb"], d_maskb),
                         (t["ut01f"], d_ut01f)):
            cld.inc(sp.dma_start(out=dst[:, :], in_=src.ap()), 16)
    cx.sec("sp", f_sp)

    def f_ve(ve):
        ve.memset(t["cst"][:, 0:1], EPS)
        ve.memset(t["cst"][:, 1:2], 1.0)
        ve.memset(t["cst"][:, 2:3], -1e30)
        ve.memset(t["cst"][:, 3:4], 0.0)
        ve.memset(t["ones_bf"][:, :], 1.0)
        ve.memset(t["ones_bf_t"][:, :], 1.0)
        cx.s["ve"].inc(ve.memset(t["ones_row"][:, :], 1.0))
    cx.sec("ve", f_ve)
    return t, (d_idb, d_idf, d_mcum, d_maskb)


def emit_ln(cx, xT, xn, cc, banksF, xn_out=None):
    s = cx.s
    lnld = [cx.sem("lnld0"), cx.sem("lnld1")]
    xnst = cx.sem("xnst")
    xbuf = [cx.sb(f"ln_x{i}", [128, TE], F32) for i in range(2)]
    xcst = [cx.sb(f"ln_xb{i}", [128, TE], BF16) for i in range(2)]
    xsq = [cx.sb(f"ln_xs{i}", [128, TE], BF16) for i in range(2)]
    mrow = cx.sb("ln_mrow", [1, TE], F32)
    msq = cx.sb("ln_msq", [1, TE], F32)
    vrow = cx.sb("ln_vrow", [1, TE], F32)
    srow = cx.sb("ln_srow", [1, TE], F32)
    rrow = cx.sb("ln_rrow", [1, TE], F32)
    mbc = cx.sb("ln_mbc", [128, TE], F32)
    rbc = cx.sb("ln_rbc", [128, TE], F32)

    ld_at = [0] * 32
    cast_at = [0] * 32
    mm_at = [0] * 32

    for k in range(32):
        def f_sp(sp, k=k):
            if k >= 2:
                sp.wait_ge(s["ac"].h, cast_at[k - 2])
            ld_at[k] = lnld[k % 2].inc(
                sp.dma_start(out=xbuf[k % 2][:, :], in_=xT.ap()[k * 128:(k + 1) * 128, :]), 16)
        cx.sec("sp", f_sp)

        def f_ac(sc, k=k):
            sc.wait_ge(lnld[k % 2].h, ld_at[k])
            if k >= 2:
                sc.wait_ge(s["pe"].h, mm_at[k - 2])
            sc.activation(xcst[k % 2][:, :], xbuf[k % 2][:, :], AF.Copy)
            cast_at[k] = s["ac"].inc(
                sc.activation(xsq[k % 2][:, :], xbuf[k % 2][:, :], AF.Square))
        cx.sec("ac", f_ac)

        def f_pe(te, k=k):
            te.wait_ge(s["ac"].h, cast_at[k])
            for j, (a, b) in enumerate(SPL):
                te.matmul(banksF[j][0:1, 0:b - a], cc["ones_bf"][:, :], xcst[k % 2][:, a:b],
                          start=(k == 0), stop=(k == 31))
                ins = te.matmul(banksF[3 + j][0:1, 0:b - a], cc["ones_bf"][:, :],
                                xsq[k % 2][:, a:b], start=(k == 0), stop=(k == 31))
            mm_at[k] = s["pe"].inc(ins)
        cx.sec("pe", f_pe)

    w = {}

    def f_ac2(sc):
        sc.wait_ge(s["pe"].h, mm_at[31])
        for j, (a, b) in enumerate(SPL):
            sc.activation(mrow[0:1, a:b], banksF[j][0:1, 0:b - a], AF.Copy, scale=1.0 / D)
            w["mv"] = s["ac"].inc(
                sc.activation(vrow[0:1, a:b], banksF[3 + j][0:1, 0:b - a], AF.Copy, scale=1.0 / D))
        sc.wait_ge(s["ac"].h, w["mv"])
        w["msq"] = s["ac"].inc(sc.activation(msq[0:1, :], mrow[0:1, :], AF.Square))
    cx.sec("ac", f_ac2)

    def f_ve2(ve):
        ve.wait_ge(s["ac"].h, w["msq"])
        w["var"] = s["ve"].inc(ve.tensor_sub(vrow[0:1, :], vrow[0:1, :], msq[0:1, :]))
    cx.sec("ve", f_ve2)

    def f_ac3(sc):
        sc.wait_ge(s["ve"].h, w["var"])
        w["sqrt"] = s["ac"].inc(
            sc.activation(srow[0:1, :], vrow[0:1, :], AF.Sqrt, bias=cc["cst"][0:1, 0:1]))
    cx.sec("ac", f_ac3)

    def f_ve3(ve):
        ve.wait_ge(s["ac"].h, w["sqrt"])
        w["rr"] = s["ve"].inc(ve.reciprocal(rrow[0:1, :], srow[0:1, :]))
    cx.sec("ve", f_ve3)

    bc_at = [0] * 3
    bcev_at = [0] * 3
    for j, (a, b) in enumerate(SPL):
        def f_pe2(te, j=j, a=a, b=b):
            te.wait_ge(s["ve"].h, w["rr"] if j == 0 else bcev_at[j - 1])
            te.matmul(banksF[0][:, 0:b - a], cc["ones_row"][:, :], mrow[0:1, a:b],
                      start=True, stop=True)
            bc_at[j] = s["pe"].inc(
                te.matmul(banksF[1][:, 0:b - a], cc["ones_row"][:, :], rrow[0:1, a:b],
                          start=True, stop=True))
        cx.sec("pe", f_pe2)

        def f_ve4(ve, j=j, a=a, b=b):
            ve.wait_ge(s["pe"].h, bc_at[j])
            ve.tensor_copy(mbc[:, a:b], banksF[0][:, 0:b - a])
            bcev_at[j] = s["ve"].inc(ve.tensor_copy(rbc[:, a:b], banksF[1][:, 0:b - a]))
        cx.sec("ve", f_ve4)

    ld2 = [0] * 32
    sub_at = [0] * 32
    app_at = [0] * 32
    for k in range(32):
        def f_sp2(sp, k=k):
            if k == 0:
                sp.wait_ge(s["ve"].h, bcev_at[2])
            if k >= 2:
                sp.wait_ge(s["ve"].h, app_at[k - 2])
            ld2[k] = lnld[k % 2].inc(
                sp.dma_start(out=xbuf[k % 2][:, :], in_=xT.ap()[k * 128:(k + 1) * 128, :]), 16)
        cx.sec("sp", f_sp2)

        def f_ve5(ve, k=k):
            ve.wait_ge(lnld[k % 2].h, ld2[k])
            sub_at[k] = s["ve"].inc(ve.tensor_sub(xbuf[k % 2][:, :], xbuf[k % 2][:, :], mbc[:, :]))
            ve.wait_ge(s["ve"].h, sub_at[k])
            app_at[k] = s["ve"].inc(
                ve.tensor_mul(xn[:, k * TE:(k + 1) * TE], xbuf[k % 2][:, :], rbc[:, :]))
        cx.sec("ve", f_ve5)

    if xn_out is not None:
        def f_gp(gp):
            gp.wait_ge(s["ve"].h, app_at[31])
            xnst.inc(gp.dma_start(
                out=xn_out.ap().rearrange("(kk p) t -> p kk t", p=128),
                in_=xn[:, :].rearrange("p (kk t) -> p kk t", kk=32)), 16)
        cx.sec("gp", f_gp)
    return app_at[31]


def emit_wtile_mm(cx, wdram, ti, ncols, wf3, wb3, xn, banksF, grp, splits,
                  prev_free, casts):
    s = cx.s
    out = {}
    b = ti % 3
    wf, wb = wf3[b], wb3[b]
    wsem = cx.sem(f"wld{b}")

    def f_sp(sp):
        if ti - 3 in casts:
            sp.wait_ge(s["po"].h, casts[ti - 3][0])
            sp.wait_ge(s["ve"].h, casts[ti - 3][1])
        out["ld"] = wsem.inc(
            sp.dma_start(out=wf[:, :], in_=wdram.ap()[ti * 128:(ti + 1) * 128, :]), 16)
    cx.sec("sp", f_sp)

    def f_po(gp):
        gp.wait_ge(wsem.h, out["ld"])
        if ti - 3 in casts:
            gp.wait_ge(s["pe"].h, casts[ti - 3][2])
        out["cast"] = s["po"].inc(gp.tensor_copy(wb[:, 0:2048], wf[:, 0:2048]))
    cx.sec("po", f_po)

    def f_vc(ve):
        ve.wait_ge(wsem.h, out["ld"])
        if ti - 3 in casts:
            ve.wait_ge(s["pe"].h, casts[ti - 3][2])
        out["castv"] = s["ve"].inc(ve.tensor_copy(wb[:, 2048:4096], wf[:, 2048:4096]))
    cx.sec("ve", f_vc)

    def f_pe(te):
        te.wait_ge(s["po"].h, out["cast"])
        te.wait_ge(s["ve"].h, out["castv"])
        if prev_free is not None:
            sem, val = prev_free
            te.wait_ge(sem.h, val)
        nsp = len(splits)
        ins = None
        for kk in range(32):
            for j, (a, b2) in enumerate(splits):
                ins = te.matmul(banksF[grp * nsp + j][0:ncols, 0:b2 - a],
                                wb[:, kk * 128:kk * 128 + ncols],
                                xn[:, kk * TE + a:kk * TE + b2],
                                start=(kk == 0), stop=(kk == 31))
        out["mm"] = s["pe"].inc(ins)
        casts[ti] = (out["cast"], out["castv"], out["mm"])
    cx.sec("pe", f_pe)
    return out


def build_pre():
    cx = Ctx()
    nc = cx.nc

    xT = cx.din("xT", [D, TE], F32)
    w_pre = cx.din("w_pre", [41 * 128, 4096], F32)
    d_convw = cx.din("convw", [128, 160], F32)
    d_convb = cx.din("convb", [128, 40], F32)
    d_dtb = cx.din("dt_bias", [64, 1], F32)
    d_alog = cx.din("a_log", [64, 1], F32)

    xn_e = cx.dout("xn_e", [D, TE], BF16)
    xh_e = cx.dout("xh_e", [NCH * 128, 32 * 128], BF16)
    b_e = cx.dout("b_e", [NCH * 128, 8 * 128], BF16)
    u_e = cx.dout("u_e", [T, H * P], BF16)
    ch_e = cx.dout("ch_e", [NCH * N, H * P], BF16)
    sl_e = cx.dout("sl_e", [N, H * P], F32)
    slog_e = cx.dout("slog_e", [1, 64], F32)
    catm_e = cx.dout("catm_e", [128, 512], F32)
    atmf_e = cx.dout("atmf_e", [128, 512], F32)
    logcf_e = cx.dout("logcf_e", [128, 512], F32)

    cx.open_block()
    s = cx.s
    cc, _ = emit_common_consts(cx)

    banksF = [cx.ps(f"bF{i}", [128, 512], F32) for i in range(6)]
    banksB = [cx.ps(f"bB{i}", [128, 1024], BF16) for i in range(2)]

    cw = cx.sb("cw", [128, 160], F32)
    cb = cx.sb("cb", [128, 40], F32)
    dtb = cx.sb("dtb", [64, 1], F32)
    alog = cx.sb("alog", [64, 1], F32)
    aneg = cx.sb("aneg", [64, 1], F32)
    xn = cx.sb("xn_arena", [128, 32 * TE], BF16)

    w = {}

    sld = cx.sem("sld")

    def f_sp0(sp):
        for dst, src in ((cw, d_convw), (cb, d_convb), (dtb, d_dtb), (alog, d_alog)):
            sld.inc(sp.dma_start(out=dst[:, :], in_=src.ap()), 16)
        w["small_ld"] = sld.n
    cx.sec("sp", f_sp0)

    def f_ac0(sc):
        sc.wait_ge(sld.h, w["small_ld"])
        sc.wait_ge(cx.sem("cld").h, cx.sem("cld").n)
        w["an1"] = s["ac"].inc(sc.activation(aneg[:, :], alog[:, :], AF.Exp))
        sc.wait_ge(s["ac"].h, w["an1"])
        w["aneg"] = s["ac"].inc(sc.activation(aneg[:, :], aneg[:, :], AF.Copy, scale=-1.0))
    cx.sec("ac", f_ac0)

    cx.barrier()
    cx.push()
    emit_ln(cx, xT, xn, cc, banksF, xn_out=xn_e)
    cx.barrier()
    cx.pop()

    dt_p = cx.sb("dt_p", [64, T], F32)
    a_bf = cx.sb("a_bf", [64, T], BF16)
    cx.push()
    wf = [cx.sb(f"wf{i}", [128, 4096], F32) for i in range(3)]
    wb = [cx.sb(f"wb{i}", [128, 4096], BF16) for i in range(3)]
    ext = [cx.sb(f"ext{i}", [128, TE], F32) for i in range(2)]
    cacc = [cx.sb(f"cacc{i}", [128, T], F32) for i in range(2)]
    obuf = [cx.sb(f"obuf{i}", [128, T], BF16) for i in range(2)]
    dt_ext = cx.sb("dt_ext", [64, TE], F32)
    sp1 = cx.sb("sp1", [64, T], F32)
    sp2 = cx.sb("sp2", [64, T], F32)
    sp3 = cx.sb("sp3", [64, T], F32)

    free_at = [None, None]
    casts = {}
    st_at = {}
    xh_st_last = [0]

    for ti in range(41):
        ncols = 64 if ti == 40 else 128
        g = ti % 2
        r = emit_wtile_mm(cx, w_pre, ti, ncols, wf, wb, xn, banksF,
                          g, SPL, free_at[g], casts)

        if ti < 40:
            def f_ac0(sc, ti=ti, g=g, r=r):
                sc.wait_ge(s["pe"].h, r["mm"])
                if ti >= 2:
                    sc.wait_ge(s["ve"].h, st_at[("conv", ti - 2)])
                ins = None
                for j, (a, b) in enumerate(SPL):
                    ins = sc.activation(ext[g][:, a:b], banksF[g * 3 + j][:, 0:b - a], AF.Copy)
                st_at[("ev", ti)] = s["ac"].inc(ins)
                free_at[g] = (s["ac"], st_at[("ev", ti)])
            cx.sec("ac", f_ac0)

            def f_ve(ve, ti=ti, g=g, r=r):
                ve.wait_ge(s["ac"].h, st_at[("ev", ti)])
                if ti >= 2:
                    ve.wait_ge(s["ac"].h, st_at[("silu", ti - 2)])
                ve.tensor_scalar(cacc[g][:, :], ext[g][:, 0:T],
                                 cw[:, ti * 4:ti * 4 + 1], cb[:, ti:ti + 1],
                                 ALU.mult, ALU.add)
                for k in range(1, 4):
                    vv = s["ve"].inc(ve.engine_nop())
                    ve.wait_ge(s["ve"].h, vv)
                    ve.scalar_tensor_tensor(cacc[g][:, :], ext[g][:, k:k + T],
                                            cw[:, ti * 4 + k:ti * 4 + k + 1], cacc[g][:, :],
                                            ALU.mult, ALU.add)
                st_at[("conv", ti)] = s["ve"].inc(ve.engine_nop())
            cx.sec("ve", f_ve)

            def f_ac(sc, ti=ti, g=g):
                sc.wait_ge(s["ve"].h, st_at[("conv", ti)])
                if ti >= 2:
                    sc.wait_ge(cx.sem(f"stp{g}").h, st_at[("dma", ti - 2)])
                st_at[("silu", ti)] = s["ac"].inc(
                    sc.activation(obuf[g][:, :], cacc[g][:, :], AF.Silu))
            cx.sec("ac", f_ac)

            def f_gp(gp, ti=ti, g=g):
                gp.wait_ge(s["ac"].h, st_at[("silu", ti)])
                if ti < 32:
                    dst = xh_e.ap()[:, ti * 128:(ti + 1) * 128].rearrange("(c p) t -> p c t", p=128)
                else:
                    dst = b_e.ap()[:, (ti - 32) * 128:(ti - 31) * 128].rearrange("(c p) t -> p c t", p=128)
                st_at[("dma", ti)] = cx.sem(f"stp{g}").inc(
                    gp.dma_start(out=dst, in_=obuf[g][:, :].rearrange("p (c t) -> p c t", c=NCH)), 16)
            cx.sec("gp", f_gp)
        else:
            def f_ve(ve, g=g, r=r):
                ve.wait_ge(s["pe"].h, r["mm"])
                for j, (a, b) in enumerate(SPL):
                    ve.tensor_copy(dt_ext[:, a:b], banksF[g * 3 + j][0:64, 0:b - a])
                v = s["ve"].inc(ve.engine_nop())
                free_at[g] = (s["ve"], v)
                ve.wait_ge(s["ve"].h, v)
                w["dt_s1"] = s["ve"].inc(ve.tensor_scalar_add(sp1[:, :], dt_ext[:, 3:TE], dtb[:, :]))
                ve.wait_ge(s["ve"].h, w["dt_s1"])
                w["dt_relu"] = s["ve"].inc(ve.tensor_scalar_max(sp2[:, :], sp1[:, :], cc["cst"][0:64, 3:4]))
            cx.sec("ve", f_ve)

            def f_ac(sc):
                sc.wait_ge(s["ve"].h, w["dt_s1"])
                v = s["ac"].inc(sc.activation(sp3[:, :], sp1[:, :], AF.Abs))
                sc.wait_ge(s["ac"].h, v)
                v = s["ac"].inc(sc.activation(sp3[:, :], sp3[:, :], AF.Exp, scale=-1.0))
                sc.wait_ge(s["ac"].h, v)
                w["dt_ln"] = s["ac"].inc(
                    sc.activation(sp3[:, :], sp3[:, :], AF.Ln, bias=cc["cst"][0:64, 1:2]))
            cx.sec("ac", f_ac)

            def f_ve2(ve):
                ve.wait_ge(s["ac"].h, w["dt_ln"])
                ve.wait_ge(s["ve"].h, w["dt_relu"])
                v = s["ve"].inc(ve.tensor_add(dt_p[:, :], sp2[:, :], sp3[:, :]))
                ve.wait_ge(s["ve"].h, v)
                ve.wait_ge(s["ac"].h, w["aneg"])
                w["a_bf"] = s["ve"].inc(ve.tensor_scalar_mul(a_bf[:, :], dt_p[:, :], aneg[:, :]))
            cx.sec("ve", f_ve2)

    xh_st_last[0] = st_at[("dma", 39)]
    cx.barrier()
    cx.pop()

    xh_c = cx.sb("xh_c", [128, 32 * C], BF16)
    b_c = cx.sb("b_c", [128, 8 * C], BF16)
    xtm = cx.sb("xtm", [128, 32 * C], BF16)
    btm = cx.sb("btm", [128, 8 * C], BF16)
    atm_c = cx.sb("atm_c", [128, 64], BF16)
    dtptm = cx.sb("dtptm", [128, 64], F32)
    catm_st = cx.sb("catm_st", [128, 512], F32)
    atmf_st = cx.sb("atmf_st", [128, 512], F32)
    logcf_st = cx.sb("logcf_st", [128, 512], F32)
    logcf_run = cx.sb("logcf_run", [128, 128], F32)
    sumcol = cx.sb("sumcol", [64, 1], F32)
    sumrow = cx.sb("sumrow", [1, 64], F32)
    sumacc = cx.sb("sumacc", [64, 2], F32)
    elast = cx.sb("elast", [128, 64], F32)
    wtm = cx.sb("wtm", [128, 64], F32)
    u_c = cx.sb("u_c", [128, H * P], BF16)
    s_run = [cx.sb(f"s_run{i}", [128, H * P], F32) for i in range(2)]
    ch_bf = cx.sb("ch_bf", [128, H * P], BF16)
    bsc = [cx.sb(f"bsc{i}", [128, 128], BF16) for i in range(2)]

    def f_init(ve):
        ve.memset(s_run[0][:, :], 0.0)
        ve.memset(logcf_run[:, 0:64], 0.0)
        w["scan_init"] = s["ve"].inc(ve.memset(sumacc[:, 0:1], 0.0))
    cx.sec("ve", f_init)

    prev = {"tr_done": None, "u_done": w.get("a_bf"), "ds_last": None,
            "ch_st": {}, "u_st": {}, "supd": None, "catm_ev": None,
            "sum_ev": None, "bc_used": None}

    for c in range(NCH):
        ldc = {}

        ck = cx.sem("ckld")

        def f_sp(sp, c=c):
            sp.wait_ge(cx.sem("stp0").h, cx.sem("stp0").n)
            sp.wait_ge(cx.sem("stp1").h, cx.sem("stp1").n)
            if prev["tr_done"] is not None:
                sp.wait_ge(s["pe"].h, prev["tr_done"])
            ck.inc(sp.dma_start(out=xh_c[:, :], in_=xh_e.ap()[c * 128:(c + 1) * 128, :]), 16)
            ldc["xb"] = ck.inc(
                sp.dma_start(out=b_c[:, :], in_=b_e.ap()[c * 128:(c + 1) * 128, :]), 16)
        cx.sec("sp", f_sp)

        tr_mm = {}
        tr_ev = {}
        for q in range(10):
            def f_pe(te, c=c, q=q):
                if q == 0:
                    te.wait_ge(cx.sem("ckld").h, ldc["xb"])
                    if prev["supd"] is not None:
                        te.wait_ge(s["ve"].h, prev["supd"])
                bank = q % 2
                if q >= 2:
                    te.wait_ge(s["ve"].h, tr_ev[q - 2])
                ins = None
                for jj in range(4):
                    gq = q * 4 + jj
                    if gq < 32:
                        srcap = xh_c[:, gq * C:(gq + 1) * C]
                    else:
                        gg = gq - 32
                        srcap = b_c[:, gg * C:(gg + 1) * C]
                    ins = te.transpose(banksB[bank][:, jj * 128:(jj + 1) * 128], srcap,
                                       cc["id_bf"][:, :])
                tr_mm[q] = s["pe"].inc(ins)
            cx.sec("pe", f_pe)

            def f_ve(ve, c=c, q=q):
                ve.wait_ge(s["pe"].h, tr_mm[q])
                bank = q % 2
                if q < 8:
                    dst = xtm[:, q * 512:(q + 1) * 512]
                else:
                    dst = btm[:, (q - 8) * 512:(q - 7) * 512]
                tr_ev[q] = s["ve"].inc(ve.tensor_copy(dst, banksB[bank][:, 0:512]))
            cx.sec("ve", f_ve)

        def f_pe_t2(te, c=c):
            te.wait_ge(s["ve"].h, tr_ev[8])
            tr_mm[40] = s["pe"].inc(
                te.transpose(banksB[0][:, 0:64], a_bf[0:64, c * C:(c + 1) * C],
                             cc["id_bf"][0:64, 0:64]))
            te.wait_ge(s["ve"].h, tr_ev[9])
            tr_mm[41] = s["pe"].inc(
                te.transpose(banksF[4][:, 0:64], dt_p[0:64, c * C:(c + 1) * C],
                             cc["id_f32"][0:64, 0:64]))
        cx.sec("pe", f_pe_t2)

        def f_ve_t2(ve, c=c):
            ve.wait_ge(s["pe"].h, tr_mm[40])
            tr_ev[40] = s["ve"].inc(ve.tensor_copy(atm_c[:, :], banksB[0][:, 0:64]))
            ve.wait_ge(s["pe"].h, tr_mm[41])
            tr_ev[41] = s["ve"].inc(ve.tensor_copy(dtptm[:, :], banksF[4][:, 0:64]))
        cx.sec("ve", f_ve_t2)

        tr_done = tr_mm[41]

        def f_ac1(sc, c=c):
            sc.wait_ge(s["ve"].h, tr_ev[40])
            w[f"atmf{c}"] = s["ac"].inc(
                sc.activation(atmf_st[:, c * 64:(c + 1) * 64], atm_c[:, :], AF.Copy))
        cx.sec("ac", f_ac1)

        camm = {}

        def f_pe1(te, c=c):
            te.wait_ge(s["ve"].h, tr_ev[40])
            if prev["catm_ev"] is not None:
                te.wait_ge(s["ve"].h, prev["catm_ev"])
            camm["ca"] = s["pe"].inc(
                te.matmul(banksF[0][:, 0:64], cc["m_cum"][:, :], atm_c[:, :], start=True, stop=True))
            if prev["sum_ev"] is not None:
                te.wait_ge(s["ve"].h, prev["sum_ev"])
            camm["sum"] = s["pe"].inc(
                te.matmul(banksF[1][0:64, 0:1], atm_c[:, :], cc["ones_bf"][:, :], start=True, stop=True))
        cx.sec("pe", f_pe1)

        def f_ve1(ve, c=c):
            ve.wait_ge(s["pe"].h, camm["ca"])
            camm["catm_ev"] = s["ve"].inc(
                ve.tensor_copy(catm_st[:, c * 64:(c + 1) * 64], banksF[0][:, 0:64]))
            ve.wait_ge(s["pe"].h, camm["sum"])
            camm["sum_ev"] = s["ve"].inc(ve.tensor_copy(sumcol[:, :], banksF[1][0:64, 0:1]))
            ve.wait_ge(s["ve"].h, camm["sum_ev"])
            w[f"sacc{c}"] = s["ve"].inc(
                ve.tensor_add(sumacc[:, (c + 1) % 2:(c + 1) % 2 + 1],
                              sumacc[:, c % 2:c % 2 + 1], sumcol[:, :]))
        cx.sec("ve", f_ve1)

        def f_pe2(te, c=c):
            te.wait_ge(s["ve"].h, camm["sum_ev"])
            camm["srow"] = s["pe"].inc(
                te.transpose(banksF[1][0:1, 0:64], sumcol[:, :], cc["id_f32"][0:64, 0:64]))
        cx.sec("pe", f_pe2)

        def f_ve2b(ve, c=c):
            ve.wait_ge(s["pe"].h, camm["srow"])
            camm["srow_ev"] = s["ve"].inc(ve.tensor_copy(sumrow[:, :], banksF[1][0:1, 0:64]))
        cx.sec("ve", f_ve2b)

        def f_pe3(te, c=c):
            te.wait_ge(s["ve"].h, camm["srow_ev"])
            if prev["bc_used"] is not None:
                te.wait_ge(s["ve"].h, prev["bc_used"][0])
                te.wait_ge(s["ac"].h, prev["bc_used"][1])
            camm["bc"] = s["pe"].inc(
                te.matmul(banksF[2][:, 0:64], cc["ones_row"][:, :], sumrow[:, :], start=True, stop=True))
        cx.sec("pe", f_pe3)

        def f_ac2(sc, c=c):
            sc.wait_ge(s["pe"].h, camm["bc"])
            camm["elast"] = s["ac"].inc(sc.activation(elast[:, :], banksF[2][:, 0:64], AF.Exp))
        cx.sec("ac", f_ac2)

        def f_ve3b(ve, c=c):
            ve.wait_ge(s["pe"].h, camm["bc"])
            ve.tensor_copy(logcf_st[:, c * 64:(c + 1) * 64], logcf_run[:, (c % 2) * 64:(c % 2) * 64 + 64])
            ve.tensor_add(logcf_run[:, ((c + 1) % 2) * 64:((c + 1) % 2) * 64 + 64],
                          logcf_run[:, (c % 2) * 64:(c % 2) * 64 + 64], banksF[2][:, 0:64])
            ve.wait_ge(s["ve"].h, camm["catm_ev"])
            camm["wtm_sub"] = s["ve"].inc(
                ve.tensor_sub(wtm[:, :], banksF[2][:, 0:64], catm_st[:, c * 64:(c + 1) * 64]))
        cx.sec("ve", f_ve3b)

        def f_ac3(sc, c=c):
            sc.wait_ge(s["ve"].h, camm["wtm_sub"])
            camm["wtm"] = s["ac"].inc(sc.activation(wtm[:, :], wtm[:, :], AF.Exp))
        cx.sec("ac", f_ac3)

        def f_ve4b(ve, c=c):
            ve.wait_ge(s["ve"].h, tr_ev[41])
            ve.wait_ge(s["ve"].h, tr_ev[7])
            if c >= 1 and prev["u_st"].get(c - 1):
                ve.wait_ge(cx.sem("ust").h, prev["u_st"][c - 1])
            if prev["ds_last"] is not None:
                ve.wait_ge(s["pe"].h, prev["ds_last"])
            for h in range(H):
                ve.tensor_scalar_mul(u_c[:, h * P:(h + 1) * P],
                                     xtm[:, h * P:(h + 1) * P], dtptm[:, h:h + 1])
            camm["u_done"] = s["ve"].inc(ve.engine_nop())
        cx.sec("ve", f_ve4b)

        def f_gp1(gp, c=c):
            gp.wait_ge(s["ve"].h, camm["u_done"])
            prev["u_st"][c] = cx.sem("ust").inc(
                gp.dma_start(out=u_e.ap()[c * C:(c + 1) * C, :], in_=u_c[:, :]), 16)
        cx.sec("gp", f_gp1)

        def f_ve5b(ve, c=c):
            if prev["supd"] is not None:
                ve.wait_ge(s["ve"].h, prev["supd"])
            if c >= 1 and prev["ch_st"].get(c - 1):
                ve.wait_ge(cx.sem("chst").h, prev["ch_st"][c - 1])
            camm["ch_cast"] = s["ve"].inc(ve.tensor_copy(ch_bf[:, :], s_run[c % 2][:, :]))
        cx.sec("ve", f_ve5b)

        def f_gp2(gp, c=c):
            gp.wait_ge(s["ve"].h, camm["ch_cast"])
            prev["ch_st"][c] = cx.sem("chst").inc(
                gp.dma_start(out=ch_e.ap()[c * N:(c + 1) * N, :], in_=ch_bf[:, :]), 16)
        cx.sec("gp", f_gp2)

        ds_mm = {}
        bsc_at = {}
        supd_at = {}
        for h in range(H):
            def f_veh(ve, c=c, h=h):
                if h == 0:
                    ve.wait_ge(s["ac"].h, camm["wtm"])
                g = h >> 3
                if h >= 2:
                    ve.wait_ge(s["pe"].h, ds_mm[h - 2])
                bsc_at[h] = s["ve"].inc(
                    ve.tensor_scalar_mul(bsc[h % 2][:, :], btm[:, g * 128:(g + 1) * 128],
                                         wtm[:, h:h + 1]))
            cx.sec("ve", f_veh)

            def f_peh(te, c=c, h=h):
                if h == 0:
                    te.wait_ge(s["ve"].h, camm["u_done"])
                te.wait_ge(s["ve"].h, bsc_at[h])
                if h >= 2:
                    te.wait_ge(s["ve"].h, supd_at[h - 2])
                ds_mm[h] = s["pe"].inc(
                    te.matmul(banksF[3 + (h % 2) * 2][:, 0:64], bsc[h % 2][:, :],
                              u_c[:, h * P:(h + 1) * P], start=True, stop=True))
            cx.sec("pe", f_peh)

            def f_veh2(ve, c=c, h=h):
                if h == 0:
                    ve.wait_ge(s["ac"].h, camm["elast"])
                    ve.wait_ge(s["ve"].h, camm["ch_cast"])
                ve.wait_ge(s["pe"].h, ds_mm[h])
                supd_at[h] = s["ve"].inc(
                    ve.scalar_tensor_tensor(
                        s_run[(c + 1) % 2][:, h * P:(h + 1) * P],
                        s_run[c % 2][:, h * P:(h + 1) * P],
                        elast[:, h:h + 1],
                        banksF[3 + (h % 2) * 2][:, 0:64],
                        ALU.mult, ALU.add))
            cx.sec("ve", f_veh2)

        prev["tr_done"] = tr_done
        prev["supd"] = supd_at[H - 1]
        prev["ds_last"] = ds_mm[H - 1]
        prev["catm_ev"] = camm["catm_ev"]
        prev["sum_ev"] = camm["sum_ev"]
        prev["bc_used"] = (camm["wtm_sub"], camm["elast"])
        w[f"supd{c}"] = supd_at[H - 1]
        w[f"sacc_last"] = w[f"sacc{c}"]

    def f_pe9(te):
        te.wait_ge(s["ve"].h, w["sacc_last"])
        w["slog_mm"] = s["pe"].inc(
            te.transpose(banksF[1][0:1, 0:64], sumacc[0:64, 0:1], cc["id_f32"][0:64, 0:64]))
    cx.sec("pe", f_pe9)

    def f_ve9(ve):
        ve.wait_ge(s["pe"].h, w["slog_mm"])
        w["slog_ev"] = s["ve"].inc(ve.tensor_copy(sumrow[:, :], banksF[1][0:1, 0:64]))
    cx.sec("ve", f_ve9)

    def f_gp9(gp):
        fin = cx.sem("fin")
        gp.wait_ge(s["ve"].h, w[f"supd{NCH-1}"])
        fin.inc(gp.dma_start(out=sl_e.ap(), in_=s_run[0][:, :]), 16)
        gp.wait_ge(s["ac"].h, w[f"atmf{NCH-1}"])
        gp.wait_ge(s["ve"].h, w["slog_ev"])
        fin.inc(gp.dma_start(out=catm_e.ap(), in_=catm_st[:, :]), 16)
        fin.inc(gp.dma_start(out=atmf_e.ap(), in_=atmf_st[:, :]), 16)
        fin.inc(gp.dma_start(out=logcf_e.ap(), in_=logcf_st[:, :]), 16)
        fin.inc(gp.dma_start(out=slog_e.ap(), in_=sumrow[:, :]), 16)
        gp.wait_ge(fin.h, fin.n)
        for nm in ("xnst", "ust", "chst", "stp0", "stp1"):
            gp.wait_ge(cx.sem(nm).h, cx.sem(nm).n)
    cx.sec("gp", f_gp9)

    cx.close_block()
    cx.st.close()
    return cx.nc


def build_main():
    cx = Ctx()
    nc = cx.nc

    xT = cx.din("xT", [D, TE], F32)
    xn_i = cx.din("xn_i", [D, TE], BF16)
    w_gc = cx.din("w_gc", [40 * 128, 4096], F32)
    d_convw = cx.din("convw_c", [128, 32], F32)
    d_convb = cx.din("convb_c", [128, 8], F32)
    w_base = cx.din("w_base", [32 * 128, 4096], F32)
    d_bb = cx.din("bb_t", [128, 32], F32)
    w_r = cx.din("w_r", [128, 4096], F32)
    d_a2t = cx.din("a2t", [128, 4096], F32)
    d_bflat = cx.din("bflat", [128, D], F32)
    w_out = cx.din("w_out", [32 * 128, 4096], F32)
    d_nw = cx.din("nw_t", [128, 32], F32)
    d_dt = cx.din("d_t", [128, 32], F32)

    xh_i = cx.din("xh_i", [NCH * 128, 32 * 128], BF16)
    b_i = cx.din("b_i", [NCH * 128, 8 * 128], BF16)
    u_i = cx.din("u_i", [T, H * P], BF16)
    ch_i = cx.din("ch_i", [NCH * N, H * P], BF16)
    catm_i = cx.din("catm_i", [128, 512], F32)
    atmf_i = cx.din("atmf_i", [128, 512], F32)
    logcf_i = cx.din("logcf_i", [128, 512], F32)
    sall = cx.din("sall", [NCORES * 64, H * N], F32)
    slogs = cx.din("slogs", [8, 64], F32)
    m2t = cx.din("m2t", [8, 8], F32)
    mask8 = cx.din("mask8", [8, 1], F32)
    eye8 = cx.din("eye8", [8, 8], F32)
    rep2 = cx.din("rep2", [8, 128], BF16)

    out_e = cx.dout("out_e", [D, T], F32)

    dbg_y = cx.dout("dbg_y", [NCH * 128, 32 * C], F32) if DEBUG_Y else None
    dbg_mp = cx.dout("dbg_mp", [2 * 128, 32 * C], BF16) if DEBUG_Y else None
    dbg_dp = cx.dout("dbg_dp", [2 * 128, 32 * C], F32) if DEBUG_Y else None
    dbg_cbt = cx.dout("dbg_cbt", [2 * 128, 4 * C], F32) if DEBUG_Y else None
    gate_s = cx.dtmp("gate_s", [NCH * 128, 32 * 128], BF16)
    cmat_s = cx.dtmp("cmat_s", [NCH * 128, 8 * 128], BF16)
    ffn_s = cx.dtmp("ffn_s", [D, T], F32)
    sin_s = cx.dtmp("sin_s", [64, H * N], F32)

    cx.open_block()
    s = cx.s
    cc, _ = emit_common_consts(cx)

    banksF = [cx.ps(f"bF{i}", [128, 512], F32) for i in range(6)]
    banksN = [cx.ps(f"bN{i}", [128, 512], F32) for i in range(2)]

    cw = cx.sb("cw", [128, 32], F32)
    cb = cx.sb("cb", [128, 8], F32)
    nw_s = cx.sb("nw_s", [128, 32], F32)
    d_s = cx.sb("d_s", [128, 32], F32)
    bb_s = cx.sb("bb_s", [128, 32], F32)
    rep_s = cx.sb("rep_s", [8, 128], BF16)
    eye8_s = cx.sb("eye8_s", [8, 8], F32)
    m2t_s = cx.sb("m2t_s", [8, 8], F32)
    mask8_s = cx.sb("mask8_s", [8, 1], F32)
    slog_s = cx.sb("slog_s", [8, 64], F32)
    xn = cx.sb("xn_arena", [128, 32 * TE], BF16)
    yn = xn

    w = {}

    sld = cx.sem("sld")

    def f_sp0(sp):
        for dst, src in ((cw, d_convw), (cb, d_convb), (nw_s, d_nw), (d_s, d_dt),
                         (bb_s, d_bb), (rep_s, rep2), (eye8_s, eye8), (m2t_s, m2t),
                         (mask8_s, mask8), (slog_s, slogs)):
            sld.inc(sp.dma_start(out=dst[:, :], in_=src.ap()), 16)
        sld.inc(sp.dma_start(out=xn[:, :].rearrange("p (kk t) -> p kk t", kk=32),
                             in_=xn_i.ap().rearrange("(kk p) t -> p kk t", p=128)), 16)
        w["xn_ld"] = sld.n
    cx.sec("sp", f_sp0)

    cx.barrier()

    cx.push()
    coef = cx.sb("coef", [8, 64], F32)
    ccrow = cx.sb("ccrow", [1, 64], F32)
    cc_col = cx.sb("cc_col", [64, 1], F32)
    sj = [cx.sb(f"sj{i}", [64, H * N], F32) for i in range(2)]
    sacc2 = [cx.sb(f"sacc2{i}", [64, H * N], F32) for i in range(2)]

    def f_pe0(te):
        w["coef_mm"] = s["pe"].inc(
            te.matmul(banksF[0][0:8, 0:64], m2t_s[:, :], slog_s[:, :], start=True, stop=True))
    cx.sec("pe", f_pe0)

    def f_ac0(sc):
        sc.wait_ge(s["pe"].h, w["coef_mm"])
        w["coef_exp"] = s["ac"].inc(sc.activation(coef[:, :], banksF[0][0:8, 0:64], AF.Exp))
    cx.sec("ac", f_ac0)

    def f_ve0(ve):
        ve.wait_ge(s["ac"].h, w["coef_exp"])
        w["coef"] = s["ve"].inc(ve.tensor_scalar_mul(coef[:, :], coef[:, :], mask8_s[:, :]))
    cx.sec("ve", f_ve0)

    acc_at = [0] * 8
    for j in range(8):
        ldj = {}

        def f_sp(sp, j=j):
            if j >= 2:
                sp.wait_ge(s["ve"].h, acc_at[j - 2])
            ldj["x"] = cx.sem(f"sjld{j % 2}").inc(
                sp.dma_start(out=sj[j % 2][:, :], in_=sall.ap()[j * 64:(j + 1) * 64, :]), 16)
        cx.sec("sp", f_sp)

        def f_pe(te, j=j):
            te.wait_ge(s["ve"].h, w["coef"] if j == 0 else ldj.get("prev_ccev", acc_at[j - 1]))
            ldj["ccmm"] = s["pe"].inc(
                te.matmul(banksF[1][0:1, 0:64], eye8_s[:, j:j + 1], coef[:, :], start=True, stop=True))
        cx.sec("pe", f_pe)

        def f_ve(ve, j=j):
            ve.wait_ge(s["pe"].h, ldj["ccmm"])
            ldj["ccev"] = s["ve"].inc(ve.tensor_copy(ccrow[:, :], banksF[1][0:1, 0:64]))
        cx.sec("ve", f_ve)

        def f_pe2(te, j=j):
            te.wait_ge(s["ve"].h, ldj["ccev"])
            ldj["cctr"] = s["pe"].inc(
                te.transpose(banksF[2][0:64, 0:1], ccrow[:, :], cc["id_f32"][0:1, 0:1]))
        cx.sec("pe", f_pe2)

        def f_ve2(ve, j=j):
            ve.wait_ge(s["pe"].h, ldj["cctr"])
            v = s["ve"].inc(ve.tensor_copy(cc_col[:, :], banksF[2][0:64, 0:1]))
            ve.wait_ge(s["ve"].h, v)
            ve.wait_ge(cx.sem(f"sjld{j % 2}").h, ldj["x"])
            if j == 0:
                acc_at[0] = s["ve"].inc(
                    ve.tensor_scalar_mul(sacc2[1][:, :], sj[0][:, :], cc_col[:, :]))
            else:
                acc_at[j] = s["ve"].inc(
                    ve.scalar_tensor_tensor(sacc2[(j + 1) % 2][:, :], sj[j % 2][:, :],
                                            cc_col[:, :], sacc2[j % 2][:, :], ALU.mult, ALU.add))
        cx.sec("ve", f_ve2)

    def f_gp0(gp):
        gp.wait_ge(s["ve"].h, acc_at[7])
        w["sin_st"] = cx.sem("sinst").inc(gp.dma_start(out=sin_s.ap(), in_=sacc2[0][:, :]), 16)
    cx.sec("gp", f_gp0)

    cx.barrier()
    cx.pop()
    cx.push()

    wf = [cx.sb(f"wf{i}", [128, 4096], F32) for i in range(3)]
    wb = [cx.sb(f"wb{i}", [128, 4096], BF16) for i in range(3)]
    ext = [cx.sb(f"ext{i}", [128, TE], F32) for i in range(2)]
    cacc = [cx.sb(f"cacc{i}", [128, T], F32) for i in range(2)]
    obuf = [cx.sb(f"obuf{i}", [128, T], BF16) for i in range(2)]

    free_at = [None, None]
    casts = {}
    st_at = {}

    for ti in range(40):
        g = ti % 2
        r = emit_wtile_mm(cx, w_gc, ti, 128, wf, wb, xn, banksF,
                          g, SPL, free_at[g], casts)

        if ti < 32:
            def f_ac(sc, ti=ti, g=g, r=r):
                sc.wait_ge(s["pe"].h, r["mm"])
                if ti >= 2:
                    sc.wait_ge(cx.sem(f"stp{g}").h, st_at[("dma", ti - 2)])
                ins = None
                for j, (a, b) in enumerate(SPL):
                    lo = max(a, 3)
                    ins = sc.activation(obuf[g][:, lo - 3:b - 3],
                                        banksF[g * 3 + j][:, lo - a:b - a], AF.Silu)
                st_at[("silu", ti)] = s["ac"].inc(ins)
                free_at[g] = (s["ac"], st_at[("silu", ti)])
            cx.sec("ac", f_ac)

            def f_gp(gp, ti=ti, g=g):
                gp.wait_ge(s["ac"].h, st_at[("silu", ti)])
                st_at[("dma", ti)] = cx.sem(f"stp{g}").inc(
                    gp.dma_start(out=gate_s.ap()[:, ti * 128:(ti + 1) * 128]
                                 .rearrange("(c p) t -> p c t", p=128),
                                 in_=obuf[g][:, :].rearrange("p (c t) -> p c t", c=NCH)), 16)
            cx.sec("gp", f_gp)
        else:
            def f_ac0(sc, ti=ti, g=g, r=r):
                sc.wait_ge(s["pe"].h, r["mm"])
                if ti >= 34:
                    sc.wait_ge(s["ve"].h, st_at[("conv", ti - 2)])
                ins = None
                for j, (a, b) in enumerate(SPL):
                    ins = sc.activation(ext[g][:, a:b], banksF[g * 3 + j][:, 0:b - a], AF.Copy)
                st_at[("ev", ti)] = s["ac"].inc(ins)
                free_at[g] = (s["ac"], st_at[("ev", ti)])
            cx.sec("ac", f_ac0)

            def f_ve(ve, ti=ti, g=g, r=r):
                ve.wait_ge(s["ac"].h, st_at[("ev", ti)])
                if ti >= 34:
                    ve.wait_ge(s["ac"].h, st_at[("silu", ti - 2)])
                tt = ti - 32
                ve.tensor_scalar(cacc[g][:, :], ext[g][:, 0:T],
                                 cw[:, tt * 4:tt * 4 + 1], cb[:, tt:tt + 1], ALU.mult, ALU.add)
                for k in range(1, 4):
                    vv = s["ve"].inc(ve.engine_nop())
                    ve.wait_ge(s["ve"].h, vv)
                    ve.scalar_tensor_tensor(cacc[g][:, :], ext[g][:, k:k + T],
                                            cw[:, tt * 4 + k:tt * 4 + k + 1], cacc[g][:, :],
                                            ALU.mult, ALU.add)
                st_at[("conv", ti)] = s["ve"].inc(ve.engine_nop())
            cx.sec("ve", f_ve)

            def f_ac(sc, ti=ti, g=g):
                sc.wait_ge(s["ve"].h, st_at[("conv", ti)])
                if ti >= 2:
                    sc.wait_ge(cx.sem(f"stp{g}").h, st_at[("dma", ti - 2)])
                st_at[("silu", ti)] = s["ac"].inc(
                    sc.activation(obuf[g][:, :], cacc[g][:, :], AF.Silu))
            cx.sec("ac", f_ac)

            def f_gp(gp, ti=ti, g=g):
                gp.wait_ge(s["ac"].h, st_at[("silu", ti)])
                st_at[("dma", ti)] = cx.sem(f"stp{g}").inc(
                    gp.dma_start(out=cmat_s.ap()[:, (ti - 32) * 128:(ti - 31) * 128]
                                 .rearrange("(c p) t -> p c t", p=128),
                                 in_=obuf[g][:, :].rearrange("p (c t) -> p c t", c=NCH)), 16)
            cx.sec("gp", f_gp)

    gc_st_last = st_at[("dma", 39)]
    cx.barrier()

    logits = cx.sb("logits", [8, T], F32)
    ltm = cx.sb("ltm", [128, 8], F32)
    m1 = cx.sb("m1", [128, 1], F32)
    m2v = cx.sb("m2v", [128, 1], F32)
    eq1 = cx.sb("eq1", [128, 8], F32)
    eq2 = cx.sb("eq2", [128, 8], F32)
    lmask = cx.sb("lmask", [128, 8], F32)
    w1c = cx.sb("w1c", [128, 1], F32)
    w2c = cx.sb("w2c", [128, 1], F32)
    d21 = cx.sb("d21", [128, 1], F32)
    wtok = cx.sb("wtok", [128, 8], F32)
    wfm = cx.sb("wfm", [8, T], BF16)
    wrb = cx.sb("wrb", [128, 4096], BF16)
    h_sb = cx.sb("h_sb", [128, T], F32)
    hw = cx.sb("hw", [128, T], BF16)
    a2b = cx.sb("a2b", [128, 4096], BF16)
    bfl_b = cx.sb("bfl_b", [128, D], BF16)

    fld = cx.sem("fld")

    def f_sp1(sp):
        sp.wait_ge(s["po"].h, casts[39][0])
        sp.wait_ge(s["ve"].h, casts[39][1])
        sp.wait_ge(s["pe"].h, casts[39][2])
        fld.inc(sp.dma_start(out=wf[2][:, :], in_=w_r.ap()), 16)
        fld.inc(sp.dma_start(out=wf[1][:, :], in_=d_a2t.ap()), 16)
        fld.inc(sp.dma_start(out=wf[0][:, :], in_=d_bflat.ap()), 16)
        w["ffnw_ld"] = fld.n
    cx.sec("sp", f_sp1)

    def f_po1(gp):
        gp.wait_ge(fld.h, w["ffnw_ld"])
        gp.tensor_copy(wrb[:, :], wf[2][:, :])
        gp.tensor_copy(a2b[:, :], wf[1][:, :])
        w["ffnw_cast"] = s["po"].inc(gp.tensor_copy(bfl_b[:, :], wf[0][:, :]))
    cx.sec("po", f_po1)

    def f_pe1(te):
        te.wait_ge(s["po"].h, w["ffnw_cast"])
        ins = None
        for half, (a, b) in enumerate(SPL2):
            for kk in range(32):
                ins = te.matmul(banksF[half][0:8, 0:b - a], wrb[:, kk * 128:kk * 128 + 8],
                                xn[:, kk * TE + a:kk * TE + b], start=(kk == 0), stop=(kk == 31))
        w["rt_mm"] = s["pe"].inc(ins)
    cx.sec("pe", f_pe1)

    def f_ve1(ve):
        ve.wait_ge(s["pe"].h, w["rt_mm"])
        for half, (a, b) in enumerate(SPL2):
            ve.tensor_copy(logits[:, a - 3:b - 3], banksF[half][0:8, 0:b - a])
        w["rt_ev"] = s["ve"].inc(ve.engine_nop())
    cx.sec("ve", f_ve1)

    def f_pe2(te):
        te.wait_ge(s["ve"].h, w["rt_ev"])
        ins = None
        for half, (a, b) in enumerate(SPL2):
            for kk in range(32):
                ins = te.matmul(banksF[4 + half][:, 0:b - a], a2b[:, kk * 128:(kk + 1) * 128],
                                xn[:, kk * TE + a:kk * TE + b], start=(kk == 0), stop=(kk == 31))
        w["h_mm"] = s["pe"].inc(ins)
    cx.sec("pe", f_pe2)

    def f_ve2(ve):
        ve.wait_ge(s["pe"].h, w["h_mm"])
        ve.tensor_copy(h_sb[:, 0:512], banksF[4][:, 0:512])
        w["h_ev"] = s["ve"].inc(ve.tensor_copy(h_sb[:, 512:1024], banksF[5][:, 0:512]))
    cx.sec("ve", f_ve2)

    wfm_ev = [0] * NCH
    for c in range(NCH):
        tk = {}

        def f_pe(te, c=c):
            if c == 0:
                te.wait_ge(s["ve"].h, w["rt_ev"])
            else:
                te.wait_ge(s["ve"].h, wfm_ev[c - 1])
            tk["ltm"] = s["pe"].inc(
                te.transpose(banksF[2][:, 0:8], logits[:, c * C:(c + 1) * C], cc["id_f32"][0:8, 0:8]))
        cx.sec("pe", f_pe)

        def f_ve(ve, c=c):
            ve.wait_ge(s["pe"].h, tk["ltm"])
            v = s["ve"].inc(ve.tensor_copy(ltm[:, :], banksF[2][:, 0:8]))
            ve.wait_ge(s["ve"].h, v)
            v = s["ve"].inc(ve.reduce_max(m1[:, :], ltm[:, :], axis=AXX))
            ve.wait_ge(s["ve"].h, v)
            v = s["ve"].inc(ve.tensor_scalar(eq1[:, :], ltm[:, :], m1[:, :], None, ALU.is_ge))
            ve.wait_ge(s["ve"].h, v)
            v = s["ve"].inc(ve.scalar_tensor_tensor(lmask[:, :], eq1[:, :], cc["cst"][:, 2:3],
                                                    ltm[:, :], ALU.mult, ALU.add))
            ve.wait_ge(s["ve"].h, v)
            v = s["ve"].inc(ve.reduce_max(m2v[:, :], lmask[:, :], axis=AXX))
            ve.wait_ge(s["ve"].h, v)
            tk["d21"] = s["ve"].inc(ve.tensor_sub(d21[:, :], m2v[:, :], m1[:, :]))
            tk["eq2"] = s["ve"].inc(ve.tensor_scalar(eq2[:, :], lmask[:, :], m2v[:, :], None, ALU.is_ge))
        cx.sec("ve", f_ve)

        def f_ac(sc, c=c):
            sc.wait_ge(s["ve"].h, tk["d21"])
            sc.activation(w2c[:, :], d21[:, :], AF.Sigmoid)
            tk["w12"] = s["ac"].inc(sc.activation(w1c[:, :], d21[:, :], AF.Sigmoid, scale=-1.0))
        cx.sec("ac", f_ac)

        def f_ve3(ve, c=c):
            ve.wait_ge(s["ac"].h, tk["w12"])
            ve.wait_ge(s["ve"].h, tk["eq2"])
            v = s["ve"].inc(ve.tensor_scalar_mul(wtok[:, :], eq1[:, :], w1c[:, :]))
            ve.wait_ge(s["ve"].h, v)
            tk["wtok"] = s["ve"].inc(
                ve.scalar_tensor_tensor(wtok[:, :], eq2[:, :], w2c[:, :], wtok[:, :],
                                        ALU.mult, ALU.add))
        cx.sec("ve", f_ve3)

        def f_pe2b(te, c=c):
            te.wait_ge(s["ve"].h, tk["wtok"])
            tk["wfm_mm"] = s["pe"].inc(
                te.transpose(banksF[3][0:8, 0:128], wtok[:, :], cc["id_f32"][:, :]))
        cx.sec("pe", f_pe2b)

        def f_ve4(ve, c=c):
            ve.wait_ge(s["pe"].h, tk["wfm_mm"])
            wfm_ev[c] = s["ve"].inc(ve.tensor_copy(wfm[:, c * C:(c + 1) * C], banksF[3][0:8, 0:128]))
        cx.sec("ve", f_ve4)

    def f_pe3(te):
        te.wait_ge(s["ve"].h, wfm_ev[NCH - 1])
        te.wait_ge(s["ve"].h, w["h_ev"])
        te.matmul(banksF[4][:, 0:512], rep_s[:, :], wfm[:, 0:512], start=True, stop=True)
        w["wbc_mm"] = s["pe"].inc(
            te.matmul(banksF[5][:, 0:512], rep_s[:, :], wfm[:, 512:1024], start=True, stop=True))
    cx.sec("pe", f_pe3)

    def f_ve5(ve):
        ve.wait_ge(s["pe"].h, w["wbc_mm"])
        ve.tensor_mul(hw[:, 0:512], h_sb[:, 0:512], banksF[4][:, 0:512])
        w["hw"] = s["ve"].inc(ve.tensor_mul(hw[:, 512:1024], h_sb[:, 512:1024], banksF[5][:, 0:512]))
    cx.sec("ve", f_ve5)

    cx.barrier()

    ffn_o = [cx.sb(f"ffn_o{i}", [128, T], F32) for i in range(2)]
    fev = {}
    fst = {}
    free_at = [None, None]
    wcast_prev = [None]
    r_prev_castv = [0]

    casts2 = {}
    mms2 = {}
    for ti in range(32):
        g = ti % 2
        b3 = ti % 3

        def mk_extra(te, ti=ti, g=g):
            ins = None
            for half, (a, b) in enumerate(SPL2):
                ins = te.matmul(banksF[g * 3 + half][:, 0:b - a],
                                bfl_b[:, ti * 128:(ti + 1) * 128],
                                hw[:, a - 3:b - 3], start=False, stop=True)
            return ins

        r = {}

        def f_sp(sp, ti=ti, g=g, b3=b3):
            if ti < 3:
                sp.wait_ge(s["po"].h, w["ffnw_cast"])
            if ti - 3 in casts2:
                sp.wait_ge(s["po"].h, casts2[ti - 3][0])
                sp.wait_ge(s["ve"].h, casts2[ti - 3][1])
            r["ld"] = cx.sem(f"wld{b3}").inc(
                sp.dma_start(out=wf[b3][:, :], in_=w_base.ap()[ti * 128:(ti + 1) * 128, :]), 16)
        cx.sec("sp", f_sp)

        def f_po(gp, ti=ti, g=g, b3=b3):
            gp.wait_ge(cx.sem(f"wld{b3}").h, r["ld"])
            if ti - 3 in mms2:
                gp.wait_ge(s["pe"].h, mms2[ti - 3])
            r["cast"] = s["po"].inc(gp.tensor_copy(wb[b3][:, 0:2048], wf[b3][:, 0:2048]))
        cx.sec("po", f_po)

        def f_vc(ve, ti=ti, g=g, b3=b3):
            ve.wait_ge(cx.sem(f"wld{b3}").h, r["ld"])
            if ti - 3 in mms2:
                ve.wait_ge(s["pe"].h, mms2[ti - 3])
            r["castv"] = s["ve"].inc(ve.tensor_copy(wb[b3][:, 2048:4096], wf[b3][:, 2048:4096]))
        cx.sec("ve", f_vc)
        casts2[ti] = (r["cast"], r["castv"])

        def f_pe(te, ti=ti, g=g, b3=b3):
            te.wait_ge(s["po"].h, r["cast"])
            te.wait_ge(s["ve"].h, r["castv"])
            if free_at[g] is not None:
                sem, val = free_at[g]
                te.wait_ge(sem.h, val)
            ins = None
            for half, (a, b) in enumerate(SPL2):
                for kk in range(32):
                    te.matmul(banksF[g * 3 + half][:, 0:b - a], wb[b3][:, kk * 128:(kk + 1) * 128],
                              xn[:, kk * TE + a:kk * TE + b], start=(kk == 0), stop=False)
                ins = te.matmul(banksF[g * 3 + half][:, 0:b - a],
                                bfl_b[:, ti * 128:(ti + 1) * 128],
                                hw[:, a - 3:b - 3], start=False, stop=True)
            r["mm"] = s["pe"].inc(ins)
            mms2[ti] = r["mm"]
        cx.sec("pe", f_pe)

        def f_ve(ve, ti=ti, g=g):
            ve.wait_ge(s["pe"].h, r["mm"])
            if ti >= 2:
                ve.wait_ge(cx.sem(f"ffst{g}").h, fst[ti - 2])
            for half, (a, b) in enumerate(SPL2):
                ve.tensor_scalar_add(ffn_o[g][:, a - 3:b - 3],
                                     banksF[g * 3 + half][:, 0:b - a], bb_s[:, ti:ti + 1])
            fev[ti] = s["ve"].inc(ve.engine_nop())
            free_at[g] = (s["ve"], fev[ti])
        cx.sec("ve", f_ve)

        def f_gp(gp, ti=ti, g=g):
            gp.wait_ge(s["ve"].h, fev[ti])
            fst[ti] = cx.sem(f"ffst{g}").inc(
                gp.dma_start(out=ffn_s.ap()[ti * 128:(ti + 1) * 128, :], in_=ffn_o[g][:, :]), 16)
        cx.sec("gp", f_gp)

    cx.barrier()
    cx.pop()
    cx.push()

    HH = 32
    catm_s = cx.sb("catm_s", [128, 512], F32)
    atmf_s = cx.sb("atmf_s", [128, 512], F32)
    atmb_s = cx.sb("atmb_s", [128, 512], BF16)
    logcf_s = cx.sb("logcf_s", [128, 512], F32)
    elogcf_s = cx.sb("elogcf_s", [128, 512], F32)
    sinb = cx.sb("sinb", [128, H * P], BF16)
    ch_c = cx.sb("ch_c", [128, H * P], BF16)
    u_c = cx.sb("u_c", [128, H * P], BF16)
    xh_c = cx.sb("xh_c", [128, 32 * C], BF16)
    b_c = cx.sb("b_c", [128, 8 * C], BF16)
    c_c = cx.sb("c_c", [128, 8 * C], BF16)
    gate_c = cx.sb("gate_c", [128, 32 * C], BF16)
    y_fm = cx.sb("y_fm", [128, 32 * C], F32)
    dp_slab = cx.sb("dp_slab", [128, HH * C], F32)
    dm_slab = cx.sb("dm_slab", [128, HH * C], F32)
    cta_slab = cx.sb("cta_slab", [128, HH * C], BF16)
    ctb_slab = cx.sb("ctb_slab", [128, HH * C], BF16)
    mp_slab = cx.sb("mp_slab", [128, HH * C], BF16)
    cbt_sb = cx.sb("cbt_sb", [128, 4 * C], F32)
    ysq_s = cx.sb("ysq_s", [128, 32 * C], BF16)
    rst = cx.sb("rst", [1, 1024], F32)
    yt2 = cx.sb("yt2", [128, 128], F32)

    b2cl = cx.sem("b2cl")

    def f_sp2b(sp):
        sp.wait_ge(cx.sem("sinst").h, w["sin_st"])
        b2cl.inc(sp.dma_start(out=catm_s[:, :], in_=catm_i.ap()), 16)
        b2cl.inc(sp.dma_start(out=atmf_s[:, :], in_=atmf_i.ap()), 16)
        b2cl.inc(sp.dma_start(out=logcf_s[:, :], in_=logcf_i.ap()), 16)
        b2cl.inc(sp.dma_start(out=dp_slab[:, :].rearrange("p (h q) -> p h q", h=64),
                              in_=sin_s.ap().rearrange("h (p q) -> p h q", p=128)), 16)
        w["b2c_ld"] = b2cl.n
    cx.sec("sp", f_sp2b)

    def f_ve6(ve):
        ve.wait_ge(b2cl.h, w["b2c_ld"])
        ve.tensor_copy(sinb[:, :], dp_slab[:, :])
        w["sinb"] = s["ve"].inc(ve.tensor_copy(atmb_s[:, :], atmf_s[:, :]))
    cx.sec("ve", f_ve6)

    def f_ac6(sc):
        sc.wait_ge(b2cl.h, w["b2c_ld"])
        w["elogcf"] = s["ac"].inc(sc.activation(elogcf_s[:, :], logcf_s[:, :], AF.Exp))
    cx.sec("ac", f_ac6)

    yn_done = [0] * NCH
    prev_b2 = {}
    NU = 4
    UH = 16
    for c in range(NCH):
        bl = {}
        b2ld = cx.sem("b2ld")

        def f_sp(sp, c=c):
            if c >= 1:
                sp.wait_ge(s["ve"].h, prev_b2[("y_done", (c - 1, NU - 1))])
            b2ld.inc(sp.dma_start(out=ch_c[:, :], in_=ch_i.ap()[c * N:(c + 1) * N, :]), 16)
            b2ld.inc(sp.dma_start(out=u_c[:, :], in_=u_i.ap()[c * C:(c + 1) * C, :]), 16)
            b2ld.inc(sp.dma_start(out=xh_c[:, :], in_=xh_i.ap()[c * 128:(c + 1) * 128, :]), 16)
            b2ld.inc(sp.dma_start(out=b_c[:, :], in_=b_i.ap()[c * 128:(c + 1) * 128, :]), 16)
            if c == 0:
                sp.wait_ge(cx.sem("stp0").h, cx.sem("stp0").n)
                sp.wait_ge(cx.sem("stp1").h, cx.sem("stp1").n)
            b2ld.inc(sp.dma_start(out=c_c[:, :], in_=cmat_s.ap()[c * 128:(c + 1) * 128, :]), 16)
            if c >= 1:
                sp.wait_ge(s["ve"].h, yn_done[c - 1])
            b2ld.inc(sp.dma_start(out=gate_c[:, :], in_=gate_s.ap()[c * 128:(c + 1) * 128, :]), 16)
            bl["ld"] = b2ld.n
        cx.sec("sp", f_sp)

        dp_ev = {}
        cbt_ev = {}
        slab_rdy = {}
        y_mm = {}
        yev = {}

        def emit_A(u, c=c, bl=bl, dp_ev=dp_ev, cbt_ev=cbt_ev, slab_rdy=slab_rdy,
                   y_mm=y_mm, yev=yev):
            su = u % 2
            h0 = u * UH
            dps = dp_slab[:, su * UH * C:(su + 1) * UH * C]
            dms = dm_slab[:, su * UH * C:(su + 1) * UH * C]
            ctas = cta_slab[:, su * UH * C:(su + 1) * UH * C]
            ctbs = ctb_slab[:, su * UH * C:(su + 1) * UH * C]
            mps = mp_slab[:, su * UH * C:(su + 1) * UH * C]
            cbts = cbt_sb[:, su * 2 * C:(su + 1) * 2 * C]

            for q in range(UH // 4):
                def f_peQ(te, c=c, u=u, h0=h0, q=q):
                    if c == 0 and u == 0 and q == 0:
                        te.wait_ge(b2cl.h, w["b2c_ld"])
                        te.wait_ge(s["ve"].h, w["sinb"])
                    if q == 0 and u == 0:
                        te.wait_ge(b2ld.h, bl["ld"])
                    if q >= 2:
                        te.wait_ge(s["ve"].h, dp_ev[(u, q - 2)])
                    else:
                        pk = (c, u - 1) if u >= 1 else ((c - 1, NU - 1) if c >= 1 else None)
                        if pk is not None:
                            te.wait_ge(s["ve"].h, prev_b2[("dp_done", pk)])
                    ins = None
                    for jj in range(4):
                        j = q * 4 + jj
                        h = h0 + j
                        ins = te.matmul(banksF[q % 2][:, jj * C:jj * C + C],
                                        atmb_s[:, c * 64 + h:c * 64 + h + 1].broadcast_to([128, 128]),
                                        cc["m_cum"][:, :], start=True, stop=True)
                    bl[f"dpmm{u}_{q}"] = s["pe"].inc(ins)
                cx.sec("pe", f_peQ)

                def f_veQ(ve, c=c, u=u, q=q, su=su, dps=dps):
                    if q == 0:
                        pk = (c, u - 2) if u >= 2 else ((c - 1, NU - 2 + u) if c >= 1 else None)
                        if pk is not None:
                            te_cnt = prev_b2.get(("y_done_pe", pk))
                            if te_cnt:
                                ve.wait_ge(s["pe"].h, te_cnt)
                        if DEBUG_Y:
                            ve.wait_ge(cx.sem("dbgst").h, cx.sem("dbgst").n)
                    ve.wait_ge(s["pe"].h, bl[f"dpmm{u}_{q}"])
                    dp_ev[(u, q)] = s["ve"].inc(
                        ve.tensor_copy(dps[:, q * 512:(q + 1) * 512], banksF[q % 2][:, 0:512]))
                cx.sec("ve", f_veQ)
            prev_b2[("dp_done", (c, u))] = dp_ev[(u, UH // 4 - 1)]

            for gq in range(2):
                def f_peC(te, c=c, u=u, gq=gq):
                    g = u * 2 + gq
                    pk = (c, u - 1) if u >= 1 else ((c - 1, NU - 1) if c >= 1 else None)
                    if pk is not None and gq == 0:
                        te.wait_ge(s["ve"].h, prev_b2[("cbt_done", pk)])
                    bl[f"cbt{u}_{gq}"] = s["pe"].inc(
                        te.matmul(banksF[2 + gq][:, 0:128], b_c[:, g * 128:(g + 1) * 128],
                                  c_c[:, g * 128:(g + 1) * 128], start=True, stop=True))
                cx.sec("pe", f_peC)

                def f_veCb(ve, c=c, u=u, gq=gq, cbts=cbts):
                    ve.wait_ge(s["pe"].h, bl[f"cbt{u}_{gq}"])
                    cbt_ev[(u, gq)] = s["ve"].inc(
                        ve.tensor_mul(cbts[:, gq * C:(gq + 1) * C],
                                      banksF[2 + gq][:, 0:128], cc["ut01f"][:, :]))
                cx.sec("ve", f_veCb)
            prev_b2[("cbt_done", (c, u))] = cbt_ev[(u, 1)]

            def f_veDm(ve, c=c, u=u, h0=h0, dps=dps, dms=dms):
                ve.wait_ge(s["ve"].h, dp_ev[(u, UH // 4 - 1)])
                catm_rep = catm_s[:, c * 64 + h0:c * 64 + h0 + UH].unsqueeze(2).broadcast_to([128, UH, 128])
                v = s["ve"].inc(
                    ve.tensor_tensor(dms[:, :].rearrange("p (j t) -> p j t", j=UH),
                                     dps[:, :].rearrange("p (j t) -> p j t", j=UH),
                                     catm_rep, ALU.subtract))
                ve.wait_ge(s["ve"].h, v)
                bl[f"dm{u}"] = s["ve"].inc(
                    ve.tensor_single_scalar(dms[:, :], dms[:, :], 0.0, ALU.min))
            cx.sec("ve", f_veDm)

            def f_acA(sc, c=c, u=u, dps=dps, dms=dms):
                if c == 0 and u == 0:
                    sc.wait_ge(s["ac"].h, w["elogcf"])
                sc.wait_ge(s["ve"].h, bl[f"dm{u}"])
                sc.activation(dps[:, :], dps[:, :], AF.Exp)
                bl[f"exp{u}"] = s["ac"].inc(sc.activation(dms[:, :], dms[:, :], AF.Exp))
            cx.sec("ac", f_acA)

            def f_poB(gp, c=c, u=u, h0=h0, dps=dps, ctas=ctas, ctbs=ctbs):
                gp.wait_ge(s["ac"].h, bl[f"exp{u}"])
                pk = (c, u - 2) if u >= 2 else ((c - 1, NU - 2 + u) if c >= 1 else None)
                if pk is not None and prev_b2.get(("y_done_pe", pk)):
                    gp.wait_ge(s["pe"].h, prev_b2[("y_done_pe", pk)])
                c_rep = c_c[:, u * 2 * 128:(u * 2 + 2) * 128].rearrange(
                    "p (g t) -> p g t", g=2).unsqueeze(2).broadcast_to([128, 2, 8, 128])
                v1 = s["po"].inc(
                    gp.tensor_tensor(ctas[:, :].rearrange("p (g j t) -> p g j t", g=2, j=8),
                                     dps[:, :].rearrange("p (g j t) -> p g j t", g=2, j=8),
                                     c_rep, ALU.mult))
                gp.wait_ge(s["po"].h, v1)
                elog_rep = elogcf_s[:, c * 64 + h0:c * 64 + h0 + UH].unsqueeze(2).broadcast_to([128, UH, 128])
                bl[f"cta{u}"] = s["po"].inc(
                    gp.tensor_tensor(ctbs[:, :].rearrange("p (j t) -> p j t", j=UH),
                                     ctas[:, :].rearrange("p (j t) -> p j t", j=UH),
                                     elog_rep, ALU.mult))
            cx.sec("po", f_poB)

            def f_veB(ve, c=c, u=u, dms=dms, mps=mps, cbts=cbts):
                ve.wait_ge(s["ac"].h, bl[f"exp{u}"])
                ve.wait_ge(s["ve"].h, cbt_ev[(u, 1)])
                cbt_rep = cbts[:, :].rearrange("p (g t) -> p g t", g=2).unsqueeze(2).broadcast_to([128, 2, 8, 128])
                v3 = s["ve"].inc(
                    ve.tensor_tensor(mps[:, :].rearrange("p (g j t) -> p g j t", g=2, j=8),
                                     dms[:, :].rearrange("p (g j t) -> p g j t", g=2, j=8),
                                     cbt_rep, ALU.mult))
                slab_rdy[u] = v3
            cx.sec("ve", f_veB)

        def emit_Y(u, c=c, bl=bl, slab_rdy=slab_rdy, y_mm=y_mm, yev=yev):
            su = u % 2
            h0 = u * UH
            ctas = cta_slab[:, su * UH * C:(su + 1) * UH * C]
            ctbs = ctb_slab[:, su * UH * C:(su + 1) * UH * C]
            mps = mp_slab[:, su * UH * C:(su + 1) * UH * C]
            for pi in range(UH // 2):
                def f_peP(te, c=c, u=u, h0=h0, pi=pi, ctas=ctas, ctbs=ctbs, mps=mps):
                    if pi == 0:
                        te.wait_ge(s["ve"].h, slab_rdy[u])
                        te.wait_ge(s["po"].h, bl[f"cta{u}"])
                    bank = 4 + pi % 2
                    if pi >= 2:
                        te.wait_ge(s["ve"].h, yev[(u, pi - 2)])
                    else:
                        pk = (c, u - 1) if u >= 1 else ((c - 1, NU - 1) if c >= 1 else None)
                        if pk is not None:
                            te.wait_ge(s["ve"].h, prev_b2[("y_done", pk)])
                    h = h0 + 2 * pi
                    j = 2 * pi
                    ins = None
                    for sub in range(2):
                        hq = h + sub
                        jq = j + sub
                        pslice = banksF[bank][sub * 64:sub * 64 + 64, 0:128]
                        te.matmul(pslice, u_c[:, hq * P:(hq + 1) * P],
                                  mps[:, jq * 128:(jq + 1) * 128], start=True, stop=False)
                        te.matmul(pslice, ch_c[:, hq * P:(hq + 1) * P],
                                  ctas[:, jq * 128:(jq + 1) * 128], start=False, stop=False)
                        ins = te.matmul(pslice, sinb[:, hq * P:(hq + 1) * P],
                                        ctbs[:, jq * 128:(jq + 1) * 128], start=False, stop=True)
                    y_mm[(u, pi)] = s["pe"].inc(ins)
                cx.sec("pe", f_peP)

                def f_veP(ve, c=c, u=u, pi=pi):
                    bank = 4 + pi % 2
                    pair = u * (UH // 2) + pi
                    ve.wait_ge(s["pe"].h, y_mm[(u, pi)])
                    yev[(u, pi)] = s["ve"].inc(
                        ve.scalar_tensor_tensor(
                            y_fm[:, pair * C:(pair + 1) * C],
                            xh_c[:, pair * C:(pair + 1) * C],
                            d_s[:, pair:pair + 1],
                            banksF[bank][:, 0:128], ALU.mult, ALU.add))
                cx.sec("ve", f_veP)
            prev_b2[("y_done", (c, u))] = yev[(u, UH // 2 - 1)]
            prev_b2[("y_done_pe", (c, u))] = y_mm[(u, UH // 2 - 1)]

        emit_A(0)
        emit_A(1)
        emit_Y(0)
        emit_A(2)
        emit_Y(1)
        emit_A(3)
        emit_Y(2)
        emit_Y(3)

        if DEBUG_Y:
            def f_gpD(gp, c=c):
                gp.wait_ge(s["ve"].h, yev[(NU - 1, UH // 2 - 1)])
                cx.sem("dbgst").inc(
                    gp.dma_start(out=dbg_y.ap()[c * 128:(c + 1) * 128, :], in_=y_fm[:, :]), 16)
            cx.sec("gp", f_gpD)

        def f_acN(sc, c=c):
            sc.wait_ge(s["ve"].h, yev[(NU - 1, UH // 2 - 1)])
            if c >= 1:
                sc.wait_ge(s["pe"].h, prev_b2[("gnmm", c - 1)])
            bl["ysq"] = s["ac"].inc(sc.activation(ysq_s[:, :], y_fm[:, :], AF.Square))
        cx.sec("ac", f_acN)

        gn_mm = [0] * 32

        def f_peN(te, c=c):
            te.wait_ge(s["ac"].h, bl["ysq"])
            if c >= 1:
                te.wait_ge(s["ve"].h, prev_b2[("norm_done", c - 1)])
            for t in range(32):
                gq = t // 4
                bank = gq // 4
                gn_mm[t] = s["pe"].inc(
                    te.matmul(banksN[bank][0:1, (gq % 4) * C:(gq % 4) * C + C],
                              cc["ones_bf"][:, :], ysq_s[:, t * C:(t + 1) * C],
                              start=(t % 4 == 0), stop=(t % 4 == 3)))
        cx.sec("pe", f_peN)

        def f_acN2(sc, c=c):
            sc.wait_ge(s["pe"].h, gn_mm[31])
            sc.activation(rst[0:1, 0:512], banksN[0][0:1, 0:512], AF.Sqrt,
                          bias=cc["cst"][0:1, 0:1], scale=1.0 / 512)
            bl["gn_sq"] = s["ac"].inc(
                sc.activation(rst[0:1, 512:1024], banksN[1][0:1, 0:512], AF.Sqrt,
                              bias=cc["cst"][0:1, 0:1], scale=1.0 / 512))
        cx.sec("ac", f_acN2)

        def f_veN(ve, c=c):
            ve.wait_ge(s["ac"].h, bl["gn_sq"])
            bl["gn_rc"] = s["ve"].inc(ve.reciprocal(rst[0:1, :], rst[0:1, :]))
        cx.sec("ve", f_veN)

        gn_ap = [0] * 8
        for gq in range(8):
            def f_peG(te, c=c, gq=gq):
                te.wait_ge(s["ve"].h, bl["gn_rc"])
                if gq >= 2:
                    te.wait_ge(s["ve"].h, gn_ap[gq - 2])
                bl[f"gn_bc{gq}"] = s["pe"].inc(
                    te.matmul(banksN[gq % 2][:, 0:128], cc["ones_row"][:, :],
                              rst[0:1, gq * 128:(gq + 1) * 128], start=True, stop=True))
            cx.sec("pe", f_peG)

            def f_veG(ve, c=c, gq=gq):
                ve.wait_ge(s["pe"].h, bl[f"gn_bc{gq}"])
                v = gn_ap[gq - 1] if gq >= 1 else 0
                for tt in range(4):
                    t = gq * 4 + tt
                    if v:
                        ve.wait_ge(s["ve"].h, v)
                    v = s["ve"].inc(
                        ve.scalar_tensor_tensor(yt2[:, :], y_fm[:, t * C:(t + 1) * C],
                                                nw_s[:, t:t + 1], banksN[gq % 2][:, 0:128],
                                                ALU.mult, ALU.mult))
                    ve.wait_ge(s["ve"].h, v)
                    v = s["ve"].inc(ve.tensor_mul(yn[:, t * TE + c * C:t * TE + (c + 1) * C],
                                                  yt2[:, :], gate_c[:, t * C:(t + 1) * C]))
                gn_ap[gq] = v
            cx.sec("ve", f_veG)

        yn_done[c] = gn_ap[7]
        prev_b2[("norm_done", c)] = gn_ap[7]
        prev_b2[("gnmm", c)] = gn_mm[31]

    cx.barrier()
    cx.pop()
    cx.push()

    wf = [cx.sb(f"wfo{i}", [128, 4096], F32) for i in range(3)]
    wb = [cx.sb(f"wbo{i}", [128, 4096], BF16) for i in range(3)]
    x_d = [cx.sb(f"x_d{i}", [128, T], F32) for i in range(2)]
    f_d = [cx.sb(f"f_d{i}", [128, T], F32) for i in range(2)]
    o_d = [cx.sb(f"o_d{i}", [128, T], F32) for i in range(2)]
    oev = {}
    ost = {}
    free_at = [None, None]
    casts3 = {}
    mms3 = {}

    for ti in range(32):
        g = ti % 2
        b3 = ti % 3
        ro = {}

        def f_sp(sp, ti=ti, g=g, b3=b3):
            if ti - 3 in casts3:
                sp.wait_ge(s["po"].h, casts3[ti - 3][0])
                sp.wait_ge(s["ve"].h, casts3[ti - 3][1])
            if ti >= 2:
                sp.wait_ge(s["ve"].h, oev[ti - 2])
            if ti < 2:
                sp.wait_ge(cx.sem("ffst0").h, cx.sem("ffst0").n)
                sp.wait_ge(cx.sem("ffst1").h, cx.sem("ffst1").n)
            osem = cx.sem(f"old{b3}")
            osem.inc(sp.dma_start(out=wf[b3][:, :], in_=w_out.ap()[ti * 128:(ti + 1) * 128, :]), 16)
            osem.inc(sp.dma_start(out=x_d[g][:, :], in_=xT.ap()[ti * 128:(ti + 1) * 128, 3:TE]), 16)
            osem.inc(sp.dma_start(out=f_d[g][:, :], in_=ffn_s.ap()[ti * 128:(ti + 1) * 128, :]), 16)
            ro["ld"] = osem.n
        cx.sec("sp", f_sp)

        def f_po(gp, ti=ti, g=g, b3=b3):
            gp.wait_ge(cx.sem(f"old{b3}").h, ro["ld"])
            if ti - 3 in mms3:
                gp.wait_ge(s["pe"].h, mms3[ti - 3])
            ro["cast"] = s["po"].inc(gp.tensor_copy(wb[b3][:, 0:2048], wf[b3][:, 0:2048]))
        cx.sec("po", f_po)

        def f_vc(ve, ti=ti, g=g, b3=b3):
            ve.wait_ge(cx.sem(f"old{b3}").h, ro["ld"])
            if ti - 3 in mms3:
                ve.wait_ge(s["pe"].h, mms3[ti - 3])
            ro["castv"] = s["ve"].inc(ve.tensor_copy(wb[b3][:, 2048:4096], wf[b3][:, 2048:4096]))
        cx.sec("ve", f_vc)
        casts3[ti] = (0, 0)

        def f_pe(te, ti=ti, g=g, b3=b3):
            te.wait_ge(s["po"].h, ro["cast"])
            te.wait_ge(s["ve"].h, ro["castv"])
            if free_at[g] is not None:
                sem, val = free_at[g]
                te.wait_ge(sem.h, val)
            ins = None
            for half in range(2):
                for kk in range(32):
                    ins = te.matmul(banksF[g * 3 + half][:, 0:512],
                                    wb[b3][:, kk * 128:(kk + 1) * 128],
                                    yn[:, kk * TE + half * 512:kk * TE + half * 512 + 512],
                                    start=(kk == 0), stop=(kk == 31))
            ro["mm"] = s["pe"].inc(ins)
            mms3[ti] = ro["mm"]
            casts3[ti] = (ro["cast"], ro["castv"])
        cx.sec("pe", f_pe)

        def f_ve(ve, ti=ti, g=g):
            ve.wait_ge(s["pe"].h, ro["mm"])
            if ti >= 2:
                ve.wait_ge(cx.sem(f"ostp{g}").h, ost[ti - 2])
            for half in range(2):
                ve.tensor_add(o_d[g][:, half * 512:(half + 1) * 512],
                              banksF[g * 3 + half][:, 0:512],
                              x_d[g][:, half * 512:(half + 1) * 512])
            v = s["ve"].inc(ve.engine_nop())
            free_at[g] = (s["ve"], v)
            ve.wait_ge(s["ve"].h, v)
            oev[ti] = s["ve"].inc(ve.tensor_add(o_d[g][:, :], o_d[g][:, :], f_d[g][:, :]))
        cx.sec("ve", f_ve)

        def f_gp(gp, ti=ti, g=g):
            gp.wait_ge(s["ve"].h, oev[ti])
            ost[ti] = cx.sem(f"ostp{g}").inc(
                gp.dma_start(out=out_e.ap()[ti * 128:(ti + 1) * 128, :], in_=o_d[g][:, :]), 16)
        cx.sec("gp", f_gp)

    def f_fin(gp):
        for nm in ("ostp0", "ostp1"):
            gp.wait_ge(cx.sem(nm).h, cx.sem(nm).n)
    cx.sec("gp", f_fin)

    cx.close_block()
    while len(cx.stacks) > 1:
        cx.pop()
    cx.st.close()
    return cx.nc


def tile_w(W):
    dcol = W.shape[1]
    nt = (dcol + 127) // 128
    Wp = np.zeros((D, nt * 128), np.float32)
    Wp[:, :dcol] = W
    return np.ascontiguousarray(
        Wp.reshape(32, 128, nt, 128).transpose(2, 1, 0, 3).reshape(nt * 128, 4096))


def _host_inputs(args):
    (x, w_in, convw, convb, dtb, alog, Dv, nw, w_outm, w_basem, b_base, w_rm,
     Am, Bm) = args
    f32 = np.float32
    xt_full = np.ascontiguousarray(x.reshape(L, D).T)
    w_in_t = np.ascontiguousarray(w_in.T)
    host = {}
    host["w_pre"] = tile_w(
        np.concatenate([w_in_t[:, 4096:9216], w_in_t[:, 10240:10304]], axis=1))
    host["w_gc"] = tile_w(
        np.concatenate([w_in_t[:, 0:4096], w_in_t[:, 9216:10240]], axis=1))
    host["w_base"] = tile_w(w_basem.T)
    host["w_out"] = tile_w(w_outm.T)
    host["w_r"] = tile_w(w_rm.T)
    host["a2t"] = tile_w(Am.reshape(128, D).T)
    host["bflat"] = np.ascontiguousarray(Bm.transpose(0, 2, 1).reshape(128, D))
    host["convw"] = np.ascontiguousarray(
        convw[0:5120].reshape(40, 128, 4).transpose(1, 0, 2).reshape(128, 160))
    host["convb"] = np.ascontiguousarray(convb[0:5120].reshape(40, 128).T)
    host["convw_c"] = np.ascontiguousarray(
        convw[5120:6144].reshape(8, 128, 4).transpose(1, 0, 2).reshape(128, 32))
    host["convb_c"] = np.ascontiguousarray(convb[5120:6144].reshape(8, 128).T)
    host["nw_t"] = np.ascontiguousarray(nw.reshape(32, 128).T)
    host["d_t"] = np.ascontiguousarray(np.repeat(Dv, P).reshape(32, 128).T)
    host["bb_t"] = np.ascontiguousarray(b_base.reshape(32, 128).T)
    host["dt_bias"] = np.ascontiguousarray(dtb.reshape(64, 1))
    host["a_log"] = np.ascontiguousarray(alog.reshape(64, 1))
    host["eye8"] = np.eye(8, dtype=f32)
    rep2 = np.zeros((8, 128), dtype=BF)
    for e in range(8):
        rep2[e, e * 16:(e + 1) * 16] = BF(2.0)
    host["rep2"] = rep2
    host["x_slices"] = []
    for i in range(NCORES):
        sl = np.zeros((D, TE), f32)
        lo = i * T - 3
        if lo < 0:
            sl[:, 3:] = xt_full[:, 0:T]
        else:
            sl[:, :] = xt_full[:, lo:lo + TE]
        host["x_slices"].append(np.ascontiguousarray(sl))
    return host


def kernel(L0_attn_hidden_states, L0_attn_in_proj_w, L0_attn_conv_weight,
           L0_attn_conv_bias, L0_attn_dt_bias, L0_attn_A_log, L0_attn_D,
           L0_attn_norm_weight, L0_attn_out_proj_w, L0_ffn_W_base,
           L0_ffn_b_base, L0_ffn_W_router, L0_ffn_A, L0_ffn_B):
    f32 = np.float32
    args = [np.asarray(a, f32) for a in (
        L0_attn_hidden_states, L0_attn_in_proj_w, L0_attn_conv_weight,
        L0_attn_conv_bias, L0_attn_dt_bias, L0_attn_A_log, L0_attn_D,
        L0_attn_norm_weight, L0_attn_out_proj_w, L0_ffn_W_base,
        L0_ffn_b_base, L0_ffn_W_router, L0_ffn_A, L0_ffn_B)]
    host = _host_inputs([args[0], args[1], args[2], args[3], args[4], args[5],
                         args[6], args[7], args[8], args[9], args[10], args[11],
                         args[12], args[13]])
    cnp = np_consts()

    nc1 = build_pre()
    in1 = []
    for i in range(NCORES):
        m = dict(xT=host["x_slices"][i], w_pre=host["w_pre"], convw=host["convw"],
                 convb=host["convb"], dt_bias=host["dt_bias"], a_log=host["a_log"],
                 id_bf=cnp["id_bf"], id_f32=cnp["id_f32"], m_cum=cnp["m_cum"],
                 maskb=cnp["maskb"], ut01f=cnp["ut01f"])
        in1.append(m)
    r1 = run_bass_kernel_spmd(nc1, in1, core_ids=list(range(NCORES))).results

    sall = np.ascontiguousarray(np.stack(
        [np.asarray(r1[i]["sl_e"]).reshape(128, 64, 64).transpose(1, 0, 2).reshape(64, H * N)
         for i in range(NCORES)], axis=0).reshape(NCORES * 64, H * N))
    slogs = np.ascontiguousarray(np.concatenate(
        [np.asarray(r1[i]["slog_e"])[:, 0:64] for i in range(NCORES)], axis=0))

    nc2 = build_main()
    in2 = []
    for i in range(NCORES):
        m2 = np.zeros((8, 8), f32)
        for k in range(8):
            for j in range(8):
                if j < k < i:
                    m2[k, j] = 1.0
        mask8 = (np.arange(8) < i).astype(f32).reshape(8, 1)
        m = dict(xT=host["x_slices"][i], xn_i=np.asarray(r1[i]["xn_e"]),
                 w_gc=host["w_gc"], convw_c=host["convw_c"], convb_c=host["convb_c"],
                 w_base=host["w_base"], bb_t=host["bb_t"], w_r=host["w_r"],
                 a2t=host["a2t"], bflat=host["bflat"], w_out=host["w_out"],
                 nw_t=host["nw_t"], d_t=host["d_t"],
                 xh_i=np.asarray(r1[i]["xh_e"]), b_i=np.asarray(r1[i]["b_e"]),
                 u_i=np.asarray(r1[i]["u_e"]), ch_i=np.asarray(r1[i]["ch_e"]),
                 catm_i=np.asarray(r1[i]["catm_e"]), atmf_i=np.asarray(r1[i]["atmf_e"]),
                 logcf_i=np.asarray(r1[i]["logcf_e"]),
                 sall=sall, slogs=slogs, m2t=np.ascontiguousarray(m2), mask8=mask8,
                 eye8=host["eye8"], rep2=host["rep2"],
                 id_bf=cnp["id_bf"], id_f32=cnp["id_f32"], m_cum=cnp["m_cum"],
                 maskb=cnp["maskb"], ut01f=cnp["ut01f"])
        in2.append(m)
    r2 = run_bass_kernel_spmd(nc2, in2, core_ids=list(range(NCORES))).results

    out = np.concatenate([np.asarray(r2[i]["out_e"]).T for i in range(NCORES)], axis=0)
    return np.ascontiguousarray(out.reshape(1, L, D)).astype(np.float32)


# version 08 (final)
# speedup vs baseline: 1.7076x; 1.7076x over previous
import contextlib
import numpy as np
import ml_dtypes

import concourse.bass as bass
import concourse.mybir as mybir
from concourse.bass_utils import run_bass_kernel_spmd

F32 = mybir.dt.float32
BF16 = mybir.dt.bfloat16
AF = mybir.ActivationFunctionType
ALU = mybir.AluOpType
AXX = mybir.AxisListType.X

NCORES = 8
D = 4096
L = 8192
T = 1024
TE = T + 3
NCH = 8
C = 128
H = 64
P = 64
N = 128
G = 8
EPS = 1e-5

BF = ml_dtypes.bfloat16

SPL = [(0, 343), (343, 685), (685, 1027)]
SPL2 = [(3, 515), (515, 1027)]
DEBUG_Y = False

ENG_NAMES = ("pe", "ve", "ac", "po", "sp", "gp")


def gp_tc(eng, dst, srcap):
    return eng.tensor_copy(dst, srcap)


class Sem:
    def __init__(self, st, nc, name):
        self.h = st.enter_context(nc.semaphore(name))
        self.n = 0

    def inc(self, instr, k=1):
        instr.then_inc(self.h, k)
        self.n += k
        return self.n


class Ctx:

    def __init__(self):
        self.nc = bass.Bass("TRN2", target_bir_lowering=False)
        self.st = contextlib.ExitStack()
        self.stacks = [self.st]
        self.block_cm = None
        self.block = None
        self.s = {}

    def sem(self, name):
        if name not in self.s:
            self.s[name] = Sem(self.st, self.nc, "sx_" + name)
        return self.s[name]

    def push(self):
        self.stacks.append(contextlib.ExitStack())

    def pop(self):
        self.stacks.pop().close()

    def open_block(self):
        self.block_cm = self.nc.Block()
        self.block = self.block_cm.__enter__()
        for nm in ENG_NAMES:
            self.s[nm] = Sem(self.st, self.nc, "s_" + nm)

    def close_block(self):
        self.block_cm.__exit__(None, None, None)

    def sb(self, name, shape, dt):
        return self.stacks[-1].enter_context(self.nc.sbuf_tensor(name, shape, dt))

    def ps(self, name, shape, dt):
        return self.st.enter_context(self.nc.psum_tensor(name, shape, dt))

    def din(self, name, shape, dt):
        return self.nc.dram_tensor(name, shape, dt, kind="ExternalInput")

    def dout(self, name, shape, dt):
        return self.nc.dram_tensor(name, shape, dt, kind="ExternalOutput")

    def dtmp(self, name, shape, dt):
        return self.nc.dram_tensor(name, shape, dt, kind="Internal")

    def sec(self, eng, fn):
        deco = getattr(self.block, {"pe": "tensor", "ve": "vector", "ac": "scalar",
                                    "po": "gpsimd", "sp": "sync", "gp": "gpsimd"}[eng])
        deco(fn)

    def barrier(self):
        snap = {nm: self.s[nm].n for nm in ("pe", "ve", "ac", "po")}
        def mk(eng):
            def f(e):
                e.wait_ge(self.s[eng].h, self.s[eng].n)
                if eng == "pe":
                    self.s["pe"].inc(e.nop())
                elif eng == "ve":
                    self.s["ve"].inc(e.engine_nop())
                elif eng == "ac":
                    self.s["ac"].inc(e.nop())
                elif eng == "po":
                    self.s["po"].inc(e.engine_nop())
            return f
        for eng in ("pe", "ve", "ac", "po"):
            self.sec(eng, mk(eng))
        post = {nm: self.s[nm].n for nm in ("pe", "ve", "ac", "po")}
        allsems = [(sm.h, sm.n) for nm, sm in self.s.items()
                   if nm not in ("sp", "gp")]

        def mkw(eng):
            def f(e):
                for h, nvv in allsems:
                    e.wait_ge(h, nvv)
            return f
        for eng in ("pe", "ve", "ac", "po", "sp", "gp"):
            self.sec(eng, mkw(eng))


def np_consts():
    s = np.arange(128)
    return {
        "id_bf": np.eye(128, dtype=BF),
        "id_f32": np.eye(128, dtype=np.float32),
        "m_cum": (s[:, None] <= s[None, :]).astype(BF),
        "maskb": np.where(s[:, None] > s[None, :], np.float32(-1e9),
                          np.float32(0.0)).astype(np.float32),
        "ut01f": (s[:, None] <= s[None, :]).astype(np.float32),
    }


def emit_common_consts(cx):
    nc = cx.nc
    d_idb = cx.din("id_bf", [128, 128], BF16)
    d_idf = cx.din("id_f32", [128, 128], F32)
    d_mcum = cx.din("m_cum", [128, 128], BF16)
    d_maskb = cx.din("maskb", [128, 128], F32)
    d_ut01f = cx.din("ut01f", [128, 128], F32)
    t = {
        "id_bf": cx.sb("id_bf_s", [128, 128], BF16),
        "id_f32": cx.sb("id_f32_s", [128, 128], F32),
        "m_cum": cx.sb("m_cum_s", [128, 128], BF16),
        "maskb": cx.sb("maskb_s", [128, 128], F32),
        "ut01f": cx.sb("ut01f_s", [128, 128], F32),
        "cst": cx.sb("cst", [128, 6], F32),
        "ones_bf": cx.sb("ones_bf", [128, 1], BF16),
        "ones_bf_t": cx.sb("ones_bf_t", [128, 128], BF16),
        "ones_row": cx.sb("ones_row", [1, 128], F32),
    }

    cld = cx.sem("cld")

    def f_sp(sp):
        for dst, src in ((t["id_bf"], d_idb), (t["id_f32"], d_idf),
                         (t["m_cum"], d_mcum), (t["maskb"], d_maskb),
                         (t["ut01f"], d_ut01f)):
            cld.inc(sp.dma_start(out=dst[:, :], in_=src.ap()), 16)
    cx.sec("sp", f_sp)

    def f_ve(ve):
        ve.memset(t["cst"][:, 0:1], EPS)
        ve.memset(t["cst"][:, 1:2], 1.0)
        ve.memset(t["cst"][:, 2:3], -1e30)
        ve.memset(t["cst"][:, 3:4], 0.0)
        ve.memset(t["ones_bf"][:, :], 1.0)
        ve.memset(t["ones_bf_t"][:, :], 1.0)
        cx.s["ve"].inc(ve.memset(t["ones_row"][:, :], 1.0))
    cx.sec("ve", f_ve)
    return t, (d_idb, d_idf, d_mcum, d_maskb)


def emit_ln(cx, xT, xn, cc, banksF, xn_out=None):
    s = cx.s
    lnld = [cx.sem("lnld0"), cx.sem("lnld1")]
    xnst = cx.sem("xnst")
    xbuf = [cx.sb(f"ln_x{i}", [128, TE], F32) for i in range(2)]
    xcst = [cx.sb(f"ln_xb{i}", [128, TE], BF16) for i in range(2)]
    xsq = [cx.sb(f"ln_xs{i}", [128, TE], BF16) for i in range(2)]
    mrow = cx.sb("ln_mrow", [1, TE], F32)
    msq = cx.sb("ln_msq", [1, TE], F32)
    vrow = cx.sb("ln_vrow", [1, TE], F32)
    srow = cx.sb("ln_srow", [1, TE], F32)
    rrow = cx.sb("ln_rrow", [1, TE], F32)
    mbc = cx.sb("ln_mbc", [128, TE], F32)
    rbc = cx.sb("ln_rbc", [128, TE], F32)

    ld_at = [0] * 32
    cast_at = [0] * 32
    mm_at = [0] * 32

    for k in range(32):
        def f_sp(sp, k=k):
            if k >= 2:
                sp.wait_ge(s["ac"].h, cast_at[k - 2])
            ld_at[k] = lnld[k % 2].inc(
                sp.dma_start(out=xbuf[k % 2][:, :], in_=xT.ap()[k * 128:(k + 1) * 128, :]), 16)
        cx.sec("sp", f_sp)

        def f_ac(sc, k=k):
            sc.wait_ge(lnld[k % 2].h, ld_at[k])
            if k >= 2:
                sc.wait_ge(s["pe"].h, mm_at[k - 2])
            sc.activation(xcst[k % 2][:, :], xbuf[k % 2][:, :], AF.Copy)
            cast_at[k] = s["ac"].inc(
                sc.activation(xsq[k % 2][:, :], xbuf[k % 2][:, :], AF.Square))
        cx.sec("ac", f_ac)

        def f_pe(te, k=k):
            te.wait_ge(s["ac"].h, cast_at[k])
            for j, (a, b) in enumerate(SPL):
                te.matmul(banksF[j][0:1, 0:b - a], cc["ones_bf"][:, :], xcst[k % 2][:, a:b],
                          start=(k == 0), stop=(k == 31))
                ins = te.matmul(banksF[3 + j][0:1, 0:b - a], cc["ones_bf"][:, :],
                                xsq[k % 2][:, a:b], start=(k == 0), stop=(k == 31))
            mm_at[k] = s["pe"].inc(ins)
        cx.sec("pe", f_pe)

    w = {}

    def f_ac2(sc):
        sc.wait_ge(s["pe"].h, mm_at[31])
        for j, (a, b) in enumerate(SPL):
            sc.activation(mrow[0:1, a:b], banksF[j][0:1, 0:b - a], AF.Copy, scale=1.0 / D)
            w["mv"] = s["ac"].inc(
                sc.activation(vrow[0:1, a:b], banksF[3 + j][0:1, 0:b - a], AF.Copy, scale=1.0 / D))
        sc.wait_ge(s["ac"].h, w["mv"])
        w["msq"] = s["ac"].inc(sc.activation(msq[0:1, :], mrow[0:1, :], AF.Square))
    cx.sec("ac", f_ac2)

    def f_ve2(ve):
        ve.wait_ge(s["ac"].h, w["msq"])
        w["var"] = s["ve"].inc(ve.tensor_sub(vrow[0:1, :], vrow[0:1, :], msq[0:1, :]))
    cx.sec("ve", f_ve2)

    def f_ac3(sc):
        sc.wait_ge(s["ve"].h, w["var"])
        w["sqrt"] = s["ac"].inc(
            sc.activation(srow[0:1, :], vrow[0:1, :], AF.Sqrt, bias=cc["cst"][0:1, 0:1]))
    cx.sec("ac", f_ac3)

    def f_ve3(ve):
        ve.wait_ge(s["ac"].h, w["sqrt"])
        w["rr"] = s["ve"].inc(ve.reciprocal(rrow[0:1, :], srow[0:1, :]))
    cx.sec("ve", f_ve3)

    bc_at = [0] * 3
    bcev_at = [0] * 3
    for j, (a, b) in enumerate(SPL):
        def f_pe2(te, j=j, a=a, b=b):
            te.wait_ge(s["ve"].h, w["rr"] if j == 0 else bcev_at[j - 1])
            te.matmul(banksF[0][:, 0:b - a], cc["ones_row"][:, :], mrow[0:1, a:b],
                      start=True, stop=True)
            bc_at[j] = s["pe"].inc(
                te.matmul(banksF[1][:, 0:b - a], cc["ones_row"][:, :], rrow[0:1, a:b],
                          start=True, stop=True))
        cx.sec("pe", f_pe2)

        def f_ve4(ve, j=j, a=a, b=b):
            ve.wait_ge(s["pe"].h, bc_at[j])
            ve.tensor_copy(mbc[:, a:b], banksF[0][:, 0:b - a])
            bcev_at[j] = s["ve"].inc(ve.tensor_copy(rbc[:, a:b], banksF[1][:, 0:b - a]))
        cx.sec("ve", f_ve4)

    ld2 = [0] * 32
    sub_at = [0] * 32
    app_at = [0] * 32
    for k in range(32):
        def f_sp2(sp, k=k):
            if k == 0:
                sp.wait_ge(s["ve"].h, bcev_at[2])
            if k >= 2:
                sp.wait_ge(s["ve"].h, app_at[k - 2])
            ld2[k] = lnld[k % 2].inc(
                sp.dma_start(out=xbuf[k % 2][:, :], in_=xT.ap()[k * 128:(k + 1) * 128, :]), 16)
        cx.sec("sp", f_sp2)

        def f_ve5(ve, k=k):
            ve.wait_ge(lnld[k % 2].h, ld2[k])
            sub_at[k] = s["ve"].inc(ve.tensor_sub(xbuf[k % 2][:, :], xbuf[k % 2][:, :], mbc[:, :]))
            ve.wait_ge(s["ve"].h, sub_at[k])
            app_at[k] = s["ve"].inc(
                ve.tensor_mul(xn[:, k * TE:(k + 1) * TE], xbuf[k % 2][:, :], rbc[:, :]))
        cx.sec("ve", f_ve5)

    if xn_out is not None:
        def f_gp(gp):
            gp.wait_ge(s["ve"].h, app_at[31])
            xnst.inc(gp.dma_start(
                out=xn_out.ap().rearrange("(kk p) t -> p kk t", p=128),
                in_=xn[:, :].rearrange("p (kk t) -> p kk t", kk=32)), 16)
        cx.sec("gp", f_gp)
    return app_at[31]


def emit_wtile_mm(cx, wdram, ti, ncols, wb3, xn, banksF, grp, splits,
                  prev_free, mms):
    s = cx.s
    out = {}
    b = ti % 3
    wb = wb3[b]
    wsem = cx.sem(f"wld{b}")

    def f_sp(sp):
        if ti - 3 in mms:
            sp.wait_ge(s["pe"].h, mms[ti - 3])
        out["ld"] = wsem.inc(
            sp.dma_start(out=wb[:, :], in_=wdram.ap()[ti * 128:(ti + 1) * 128, :]), 16)
    cx.sec("sp", f_sp)

    def f_pe(te):
        te.wait_ge(wsem.h, out["ld"])
        if prev_free is not None:
            sem, val = prev_free
            te.wait_ge(sem.h, val)
        nsp = len(splits)
        ins = None
        for kk in range(32):
            for j, (a, b2) in enumerate(splits):
                ins = te.matmul(banksF[grp * nsp + j][0:ncols, 0:b2 - a],
                                wb[:, kk * 128:kk * 128 + ncols],
                                xn[:, kk * TE + a:kk * TE + b2],
                                start=(kk == 0), stop=(kk == 31))
        out["mm"] = s["pe"].inc(ins)
        mms[ti] = out["mm"]
    cx.sec("pe", f_pe)
    return out


def build_pre():
    cx = Ctx()
    nc = cx.nc

    xT = cx.din("xT", [D, TE], F32)
    w_pre = cx.din("w_pre", [41 * 128, 4096], BF16)
    d_convw = cx.din("convw", [128, 160], F32)
    d_convb = cx.din("convb", [128, 40], F32)
    d_dtb = cx.din("dt_bias", [64, 1], F32)
    d_alog = cx.din("a_log", [64, 1], F32)

    xn_e = cx.dout("xn_e", [D, TE], BF16)
    xh_e = cx.dout("xh_e", [NCH * 128, 32 * 128], BF16)
    b_e = cx.dout("b_e", [NCH * 128, 8 * 128], BF16)
    u_e = cx.dout("u_e", [T, H * P], BF16)
    ch_e = cx.dout("ch_e", [NCH * N, H * P], BF16)
    sl_e = cx.dout("sl_e", [N, H * P], F32)
    slog_e = cx.dout("slog_e", [1, 64], F32)
    catm_e = cx.dout("catm_e", [128, 512], F32)
    atmf_e = cx.dout("atmf_e", [128, 512], F32)
    logcf_e = cx.dout("logcf_e", [128, 512], F32)

    cx.open_block()
    s = cx.s
    cc, _ = emit_common_consts(cx)

    banksF = [cx.ps(f"bF{i}", [128, 512], F32) for i in range(6)]
    banksB = [cx.ps(f"bB{i}", [128, 1024], BF16) for i in range(2)]

    cw = cx.sb("cw", [128, 160], F32)
    cb = cx.sb("cb", [128, 40], F32)
    dtb = cx.sb("dtb", [64, 1], F32)
    alog = cx.sb("alog", [64, 1], F32)
    aneg = cx.sb("aneg", [64, 1], F32)
    xn = cx.sb("xn_arena", [128, 32 * TE], BF16)

    w = {}

    sld = cx.sem("sld")

    def f_sp0(sp):
        for dst, src in ((cw, d_convw), (cb, d_convb), (dtb, d_dtb), (alog, d_alog)):
            sld.inc(sp.dma_start(out=dst[:, :], in_=src.ap()), 16)
        w["small_ld"] = sld.n
    cx.sec("sp", f_sp0)

    def f_ac0(sc):
        sc.wait_ge(sld.h, w["small_ld"])
        sc.wait_ge(cx.sem("cld").h, cx.sem("cld").n)
        w["an1"] = s["ac"].inc(sc.activation(aneg[:, :], alog[:, :], AF.Exp))
        sc.wait_ge(s["ac"].h, w["an1"])
        w["aneg"] = s["ac"].inc(sc.activation(aneg[:, :], aneg[:, :], AF.Copy, scale=-1.0))
    cx.sec("ac", f_ac0)

    cx.barrier()
    cx.push()
    emit_ln(cx, xT, xn, cc, banksF, xn_out=xn_e)
    cx.barrier()
    cx.pop()

    dt_p = cx.sb("dt_p", [64, T], F32)
    a_bf = cx.sb("a_bf", [64, T], BF16)
    cx.push()
    wb = [cx.sb(f"wb{i}", [128, 4096], BF16) for i in range(3)]
    ext = [cx.sb(f"ext{i}", [128, TE], F32) for i in range(2)]
    cacc = [cx.sb(f"cacc{i}", [128, T], F32) for i in range(2)]
    obuf = [cx.sb(f"obuf{i}", [128, T], BF16) for i in range(2)]
    dt_ext = cx.sb("dt_ext", [64, TE], F32)
    sp1 = cx.sb("sp1", [64, T], F32)
    sp2 = cx.sb("sp2", [64, T], F32)
    sp3 = cx.sb("sp3", [64, T], F32)

    free_at = [None, None]
    casts = {}
    st_at = {}
    xh_st_last = [0]

    for ti in range(41):
        ncols = 64 if ti == 40 else 128
        g = ti % 2
        r = emit_wtile_mm(cx, w_pre, ti, ncols, wb, xn, banksF,
                          g, SPL, free_at[g], casts)

        if ti < 40:
            def f_ve(ve, ti=ti, g=g, r=r):
                ve.wait_ge(s["pe"].h, r["mm"])
                if ti >= 2:
                    ve.wait_ge(s["ac"].h, st_at[("silu", ti - 2)])
                for j, (a, b) in enumerate(SPL):
                    ve.tensor_copy(ext[g][:, a:b], banksF[g * 3 + j][:, 0:b - a])
                v = s["ve"].inc(ve.engine_nop())
                free_at[g] = (s["ve"], v)
                ve.wait_ge(s["ve"].h, v)
                ve.tensor_scalar(cacc[g][:, :], ext[g][:, 0:T],
                                 cw[:, ti * 4:ti * 4 + 1], cb[:, ti:ti + 1],
                                 ALU.mult, ALU.add)
                for k in range(1, 4):
                    vv = s["ve"].inc(ve.engine_nop())
                    ve.wait_ge(s["ve"].h, vv)
                    ve.scalar_tensor_tensor(cacc[g][:, :], ext[g][:, k:k + T],
                                            cw[:, ti * 4 + k:ti * 4 + k + 1], cacc[g][:, :],
                                            ALU.mult, ALU.add)
                st_at[("conv", ti)] = s["ve"].inc(ve.engine_nop())
            cx.sec("ve", f_ve)

            def f_ac(sc, ti=ti, g=g):
                sc.wait_ge(s["ve"].h, st_at[("conv", ti)])
                if ti >= 2:
                    sc.wait_ge(cx.sem(f"stp{g}").h, st_at[("dma", ti - 2)])
                st_at[("silu", ti)] = s["ac"].inc(
                    sc.activation(obuf[g][:, :], cacc[g][:, :], AF.Silu))
            cx.sec("ac", f_ac)

            def f_gp(gp, ti=ti, g=g):
                gp.wait_ge(s["ac"].h, st_at[("silu", ti)])
                if ti < 32:
                    dst = xh_e.ap()[:, ti * 128:(ti + 1) * 128].rearrange("(c p) t -> p c t", p=128)
                else:
                    dst = b_e.ap()[:, (ti - 32) * 128:(ti - 31) * 128].rearrange("(c p) t -> p c t", p=128)
                st_at[("dma", ti)] = cx.sem(f"stp{g}").inc(
                    gp.dma_start(out=dst, in_=obuf[g][:, :].rearrange("p (c t) -> p c t", c=NCH)), 16)
            cx.sec("gp", f_gp)
        else:
            def f_ve(ve, g=g, r=r):
                ve.wait_ge(s["pe"].h, r["mm"])
                for j, (a, b) in enumerate(SPL):
                    ve.tensor_copy(dt_ext[:, a:b], banksF[g * 3 + j][0:64, 0:b - a])
                v = s["ve"].inc(ve.engine_nop())
                free_at[g] = (s["ve"], v)
                ve.wait_ge(s["ve"].h, v)
                w["dt_s1"] = s["ve"].inc(ve.tensor_scalar_add(sp1[:, :], dt_ext[:, 3:TE], dtb[:, :]))
                ve.wait_ge(s["ve"].h, w["dt_s1"])
                w["dt_relu"] = s["ve"].inc(ve.tensor_scalar_max(sp2[:, :], sp1[:, :], cc["cst"][0:64, 3:4]))
            cx.sec("ve", f_ve)

            def f_ac(sc):
                sc.wait_ge(s["ve"].h, w["dt_s1"])
                v = s["ac"].inc(sc.activation(sp3[:, :], sp1[:, :], AF.Abs))
                sc.wait_ge(s["ac"].h, v)
                v = s["ac"].inc(sc.activation(sp3[:, :], sp3[:, :], AF.Exp, scale=-1.0))
                sc.wait_ge(s["ac"].h, v)
                w["dt_ln"] = s["ac"].inc(
                    sc.activation(sp3[:, :], sp3[:, :], AF.Ln, bias=cc["cst"][0:64, 1:2]))
            cx.sec("ac", f_ac)

            def f_ve2(ve):
                ve.wait_ge(s["ac"].h, w["dt_ln"])
                ve.wait_ge(s["ve"].h, w["dt_relu"])
                v = s["ve"].inc(ve.tensor_add(dt_p[:, :], sp2[:, :], sp3[:, :]))
                ve.wait_ge(s["ve"].h, v)
                ve.wait_ge(s["ac"].h, w["aneg"])
                w["a_bf"] = s["ve"].inc(ve.tensor_scalar_mul(a_bf[:, :], dt_p[:, :], aneg[:, :]))
            cx.sec("ve", f_ve2)

    xh_st_last[0] = st_at[("dma", 39)]
    cx.barrier()
    cx.pop()

    xh_c = cx.sb("xh_c", [128, 32 * C], BF16)
    b_c = cx.sb("b_c", [128, 8 * C], BF16)
    xtm = cx.sb("xtm", [128, 32 * C], BF16)
    btm = cx.sb("btm", [128, 8 * C], BF16)
    atm_c = cx.sb("atm_c", [128, 64], BF16)
    dtptm = cx.sb("dtptm", [128, 64], F32)
    catm_st = cx.sb("catm_st", [128, 512], F32)
    atmf_st = cx.sb("atmf_st", [128, 512], F32)
    logcf_st = cx.sb("logcf_st", [128, 512], F32)
    logcf_run = cx.sb("logcf_run", [128, 128], F32)
    sumcol = cx.sb("sumcol", [64, 1], F32)
    sumrow = cx.sb("sumrow", [1, 64], F32)
    sumacc = cx.sb("sumacc", [64, 2], F32)
    elast = cx.sb("elast", [128, 64], F32)
    wtm = cx.sb("wtm", [128, 64], F32)
    u_c = cx.sb("u_c", [128, H * P], BF16)
    s_run = [cx.sb(f"s_run{i}", [128, H * P], F32) for i in range(2)]
    ch_bf = cx.sb("ch_bf", [128, H * P], BF16)
    bsc = [cx.sb(f"bsc{i}", [128, 128], BF16) for i in range(2)]

    def f_init(ve):
        ve.memset(s_run[0][:, :], 0.0)
        ve.memset(logcf_run[:, 0:64], 0.0)
        w["scan_init"] = s["ve"].inc(ve.memset(sumacc[:, 0:1], 0.0))
    cx.sec("ve", f_init)

    prev = {"tr_done": None, "u_done": w.get("a_bf"), "ds_last": None,
            "ch_st": {}, "u_st": {}, "supd": None, "catm_ev": None,
            "sum_ev": None, "bc_used": None}

    for c in range(NCH):
        ldc = {}

        ck = cx.sem("ckld")

        def f_sp(sp, c=c):
            sp.wait_ge(cx.sem("stp0").h, cx.sem("stp0").n)
            sp.wait_ge(cx.sem("stp1").h, cx.sem("stp1").n)
            if prev["tr_done"] is not None:
                sp.wait_ge(s["pe"].h, prev["tr_done"])
            ck.inc(sp.dma_start(out=xh_c[:, :], in_=xh_e.ap()[c * 128:(c + 1) * 128, :]), 16)
            ldc["xb"] = ck.inc(
                sp.dma_start(out=b_c[:, :], in_=b_e.ap()[c * 128:(c + 1) * 128, :]), 16)
        cx.sec("sp", f_sp)

        tr_mm = {}
        tr_ev = {}
        for gq in range(42):
            def f_pe(te, c=c, gq=gq):
                if gq == 0:
                    te.wait_ge(cx.sem("ckld").h, ldc["xb"])
                    if prev["supd"] is not None:
                        te.wait_ge(s["ve"].h, prev["supd"])
                bank = gq % 2
                if gq >= 2:
                    te.wait_ge(s["ve"].h, tr_ev[gq - 2])
                if gq < 32:
                    tr_mm[gq] = s["pe"].inc(
                        te.transpose(banksB[bank][:, 0:128], xh_c[:, gq * C:(gq + 1) * C],
                                     cc["id_bf"][:, :]))
                elif gq < 40:
                    gg = gq - 32
                    tr_mm[gq] = s["pe"].inc(
                        te.transpose(banksB[bank][:, 0:128], b_c[:, gg * C:(gg + 1) * C],
                                     cc["id_bf"][:, :]))
                elif gq == 40:
                    tr_mm[gq] = s["pe"].inc(
                        te.transpose(banksB[bank][:, 0:64], a_bf[0:64, c * C:(c + 1) * C],
                                     cc["id_bf"][0:64, 0:64]))
                else:
                    tr_mm[gq] = s["pe"].inc(
                        te.transpose(banksF[4][:, 0:64], dt_p[0:64, c * C:(c + 1) * C],
                                     cc["id_f32"][0:64, 0:64]))
            cx.sec("pe", f_pe)

            def f_ve(ve, c=c, gq=gq):
                ve.wait_ge(s["pe"].h, tr_mm[gq])
                bank = gq % 2
                if gq < 32:
                    dst = xtm[:, gq * C:(gq + 1) * C]
                    tr_ev[gq] = s["ve"].inc(ve.tensor_copy(dst, banksB[bank][:, 0:128]))
                elif gq < 40:
                    gg = gq - 32
                    tr_ev[gq] = s["ve"].inc(
                        ve.tensor_copy(btm[:, gg * C:(gg + 1) * C], banksB[bank][:, 0:128]))
                elif gq == 40:
                    tr_ev[gq] = s["ve"].inc(ve.tensor_copy(atm_c[:, :], banksB[bank][:, 0:64]))
                else:
                    tr_ev[gq] = s["ve"].inc(ve.tensor_copy(dtptm[:, :], banksF[4][:, 0:64]))
            cx.sec("ve", f_ve)
        tr_done = tr_mm[41]

        def f_ac1(sc, c=c):
            sc.wait_ge(s["ve"].h, tr_ev[40])
            w[f"atmf{c}"] = s["ac"].inc(
                sc.activation(atmf_st[:, c * 64:(c + 1) * 64], atm_c[:, :], AF.Copy))
        cx.sec("ac", f_ac1)

        camm = {}

        def f_pe1(te, c=c):
            te.wait_ge(s["ve"].h, tr_ev[40])
            if prev["catm_ev"] is not None:
                te.wait_ge(s["ve"].h, prev["catm_ev"])
            camm["ca"] = s["pe"].inc(
                te.matmul(banksF[0][:, 0:64], cc["m_cum"][:, :], atm_c[:, :], start=True, stop=True))
            if prev["sum_ev"] is not None:
                te.wait_ge(s["ve"].h, prev["sum_ev"])
            camm["sum"] = s["pe"].inc(
                te.matmul(banksF[1][0:64, 0:1], atm_c[:, :], cc["ones_bf"][:, :], start=True, stop=True))
        cx.sec("pe", f_pe1)

        def f_ve1(ve, c=c):
            ve.wait_ge(s["pe"].h, camm["ca"])
            camm["catm_ev"] = s["ve"].inc(
                ve.tensor_copy(catm_st[:, c * 64:(c + 1) * 64], banksF[0][:, 0:64]))
            ve.wait_ge(s["pe"].h, camm["sum"])
            camm["sum_ev"] = s["ve"].inc(ve.tensor_copy(sumcol[:, :], banksF[1][0:64, 0:1]))
            ve.wait_ge(s["ve"].h, camm["sum_ev"])
            w[f"sacc{c}"] = s["ve"].inc(
                ve.tensor_add(sumacc[:, (c + 1) % 2:(c + 1) % 2 + 1],
                              sumacc[:, c % 2:c % 2 + 1], sumcol[:, :]))
        cx.sec("ve", f_ve1)

        def f_pe2(te, c=c):
            te.wait_ge(s["ve"].h, camm["sum_ev"])
            camm["srow"] = s["pe"].inc(
                te.transpose(banksF[1][0:1, 0:64], sumcol[:, :], cc["id_f32"][0:64, 0:64]))
        cx.sec("pe", f_pe2)

        def f_ve2b(ve, c=c):
            ve.wait_ge(s["pe"].h, camm["srow"])
            camm["srow_ev"] = s["ve"].inc(ve.tensor_copy(sumrow[:, :], banksF[1][0:1, 0:64]))
        cx.sec("ve", f_ve2b)

        def f_pe3(te, c=c):
            te.wait_ge(s["ve"].h, camm["srow_ev"])
            if prev["bc_used"] is not None:
                te.wait_ge(s["ve"].h, prev["bc_used"][0])
                te.wait_ge(s["ac"].h, prev["bc_used"][1])
            camm["bc"] = s["pe"].inc(
                te.matmul(banksF[2][:, 0:64], cc["ones_row"][:, :], sumrow[:, :], start=True, stop=True))
        cx.sec("pe", f_pe3)

        def f_ac2(sc, c=c):
            sc.wait_ge(s["pe"].h, camm["bc"])
            camm["elast"] = s["ac"].inc(sc.activation(elast[:, :], banksF[2][:, 0:64], AF.Exp))
        cx.sec("ac", f_ac2)

        def f_ve3b(ve, c=c):
            ve.wait_ge(s["pe"].h, camm["bc"])
            ve.tensor_copy(logcf_st[:, c * 64:(c + 1) * 64], logcf_run[:, (c % 2) * 64:(c % 2) * 64 + 64])
            ve.tensor_add(logcf_run[:, ((c + 1) % 2) * 64:((c + 1) % 2) * 64 + 64],
                          logcf_run[:, (c % 2) * 64:(c % 2) * 64 + 64], banksF[2][:, 0:64])
            ve.wait_ge(s["ve"].h, camm["catm_ev"])
            camm["wtm_sub"] = s["ve"].inc(
                ve.tensor_sub(wtm[:, :], banksF[2][:, 0:64], catm_st[:, c * 64:(c + 1) * 64]))
        cx.sec("ve", f_ve3b)

        def f_ac3(sc, c=c):
            sc.wait_ge(s["ve"].h, camm["wtm_sub"])
            camm["wtm"] = s["ac"].inc(sc.activation(wtm[:, :], wtm[:, :], AF.Exp))
        cx.sec("ac", f_ac3)

        def f_ve4b(ve, c=c):
            ve.wait_ge(s["ve"].h, tr_ev[41])
            ve.wait_ge(s["ve"].h, tr_ev[31])
            if c >= 1 and prev["u_st"].get(c - 1):
                ve.wait_ge(cx.sem("ust").h, prev["u_st"][c - 1])
            if prev["ds_last"] is not None:
                ve.wait_ge(s["pe"].h, prev["ds_last"])
            for h in range(H):
                ve.tensor_scalar_mul(u_c[:, h * P:(h + 1) * P],
                                     xtm[:, h * P:(h + 1) * P], dtptm[:, h:h + 1])
            camm["u_done"] = s["ve"].inc(ve.engine_nop())
        cx.sec("ve", f_ve4b)

        def f_gp1(gp, c=c):
            gp.wait_ge(s["ve"].h, camm["u_done"])
            prev["u_st"][c] = cx.sem("ust").inc(
                gp.dma_start(out=u_e.ap()[c * C:(c + 1) * C, :], in_=u_c[:, :]), 16)
        cx.sec("gp", f_gp1)

        def f_ve5b(ve, c=c):
            if prev["supd"] is not None:
                ve.wait_ge(s["ve"].h, prev["supd"])
            if c >= 1 and prev["ch_st"].get(c - 1):
                ve.wait_ge(cx.sem("chst").h, prev["ch_st"][c - 1])
            camm["ch_cast"] = s["ve"].inc(ve.tensor_copy(ch_bf[:, :], s_run[c % 2][:, :]))
        cx.sec("ve", f_ve5b)

        def f_gp2(gp, c=c):
            gp.wait_ge(s["ve"].h, camm["ch_cast"])
            prev["ch_st"][c] = cx.sem("chst").inc(
                gp.dma_start(out=ch_e.ap()[c * N:(c + 1) * N, :], in_=ch_bf[:, :]), 16)
        cx.sec("gp", f_gp2)

        ds_mm = {}
        bsc_at = {}
        supd_at = {}
        for h in range(H):
            def f_veh(ve, c=c, h=h):
                if h == 0:
                    ve.wait_ge(s["ac"].h, camm["wtm"])
                g = h >> 3
                if h >= 2:
                    ve.wait_ge(s["pe"].h, ds_mm[h - 2])
                bsc_at[h] = s["ve"].inc(
                    ve.tensor_scalar_mul(bsc[h % 2][:, :], btm[:, g * 128:(g + 1) * 128],
                                         wtm[:, h:h + 1]))
            cx.sec("ve", f_veh)

            def f_peh(te, c=c, h=h):
                if h == 0:
                    te.wait_ge(s["ve"].h, camm["u_done"])
                te.wait_ge(s["ve"].h, bsc_at[h])
                if h >= 2:
                    te.wait_ge(s["ve"].h, supd_at[h - 2])
                ds_mm[h] = s["pe"].inc(
                    te.matmul(banksF[3 + (h % 2) * 2][:, 0:64], bsc[h % 2][:, :],
                              u_c[:, h * P:(h + 1) * P], start=True, stop=True))
            cx.sec("pe", f_peh)

            def f_veh2(ve, c=c, h=h):
                if h == 0:
                    ve.wait_ge(s["ac"].h, camm["elast"])
                    ve.wait_ge(s["ve"].h, camm["ch_cast"])
                ve.wait_ge(s["pe"].h, ds_mm[h])
                supd_at[h] = s["ve"].inc(
                    ve.scalar_tensor_tensor(
                        s_run[(c + 1) % 2][:, h * P:(h + 1) * P],
                        s_run[c % 2][:, h * P:(h + 1) * P],
                        elast[:, h:h + 1],
                        banksF[3 + (h % 2) * 2][:, 0:64],
                        ALU.mult, ALU.add))
            cx.sec("ve", f_veh2)

        prev["tr_done"] = tr_done
        prev["supd"] = supd_at[H - 1]
        prev["ds_last"] = ds_mm[H - 1]
        prev["catm_ev"] = camm["catm_ev"]
        prev["sum_ev"] = camm["sum_ev"]
        prev["bc_used"] = (camm["wtm_sub"], camm["elast"])
        w[f"supd{c}"] = supd_at[H - 1]
        w[f"sacc_last"] = w[f"sacc{c}"]

    def f_pe9(te):
        te.wait_ge(s["ve"].h, w["sacc_last"])
        w["slog_mm"] = s["pe"].inc(
            te.transpose(banksF[1][0:1, 0:64], sumacc[0:64, 0:1], cc["id_f32"][0:64, 0:64]))
    cx.sec("pe", f_pe9)

    def f_ve9(ve):
        ve.wait_ge(s["pe"].h, w["slog_mm"])
        w["slog_ev"] = s["ve"].inc(ve.tensor_copy(sumrow[:, :], banksF[1][0:1, 0:64]))
    cx.sec("ve", f_ve9)

    def f_gp9(gp):
        fin = cx.sem("fin")
        gp.wait_ge(s["ve"].h, w[f"supd{NCH-1}"])
        fin.inc(gp.dma_start(out=sl_e.ap(), in_=s_run[0][:, :]), 16)
        gp.wait_ge(s["ac"].h, w[f"atmf{NCH-1}"])
        gp.wait_ge(s["ve"].h, w["slog_ev"])
        fin.inc(gp.dma_start(out=catm_e.ap(), in_=catm_st[:, :]), 16)
        fin.inc(gp.dma_start(out=atmf_e.ap(), in_=atmf_st[:, :]), 16)
        fin.inc(gp.dma_start(out=logcf_e.ap(), in_=logcf_st[:, :]), 16)
        fin.inc(gp.dma_start(out=slog_e.ap(), in_=sumrow[:, :]), 16)
        gp.wait_ge(fin.h, fin.n)
        for nm in ("xnst", "ust", "chst", "stp0", "stp1"):
            gp.wait_ge(cx.sem(nm).h, cx.sem(nm).n)
    cx.sec("gp", f_gp9)

    cx.close_block()
    cx.st.close()
    return cx.nc


def build_main():
    cx = Ctx()
    nc = cx.nc

    xT = cx.din("xT", [D, TE], F32)
    xn_i = cx.din("xn_i", [D, TE], BF16)
    w_gc = cx.din("w_gc", [40 * 128, 4096], BF16)
    d_convw = cx.din("convw_c", [128, 32], F32)
    d_convb = cx.din("convb_c", [128, 8], F32)
    w_base = cx.din("w_base", [32 * 128, 4096], BF16)
    d_bb = cx.din("bb_t", [128, 32], F32)
    w_r = cx.din("w_r", [128, 4096], BF16)
    d_a2t = cx.din("a2t", [128, 4096], BF16)
    d_bflat = cx.din("bflat", [128, D], BF16)
    w_out = cx.din("w_out", [32 * 128, 4096], BF16)
    d_nw = cx.din("nw_t", [128, 32], F32)
    d_dt = cx.din("d_t", [128, 32], F32)

    xh_i = cx.din("xh_i", [NCH * 128, 32 * 128], BF16)
    b_i = cx.din("b_i", [NCH * 128, 8 * 128], BF16)
    u_i = cx.din("u_i", [T, H * P], BF16)
    ch_i = cx.din("ch_i", [NCH * N, H * P], BF16)
    catm_i = cx.din("catm_i", [128, 512], F32)
    atmf_i = cx.din("atmf_i", [128, 512], F32)
    logcf_i = cx.din("logcf_i", [128, 512], F32)
    sall = cx.din("sall", [NCORES * 64, H * N], F32)
    slogs = cx.din("slogs", [8, 64], F32)
    m2t = cx.din("m2t", [8, 8], F32)
    mask8 = cx.din("mask8", [8, 1], F32)
    eye8 = cx.din("eye8", [8, 8], F32)
    rep2 = cx.din("rep2", [8, 128], BF16)

    out_e = cx.dout("out_e", [D, T], F32)

    dbg_y = cx.dout("dbg_y", [NCH * 128, 32 * C], F32) if DEBUG_Y else None
    dbg_mp = cx.dout("dbg_mp", [2 * 128, 32 * C], BF16) if DEBUG_Y else None
    dbg_dp = cx.dout("dbg_dp", [2 * 128, 32 * C], F32) if DEBUG_Y else None
    dbg_cbt = cx.dout("dbg_cbt", [2 * 128, 4 * C], F32) if DEBUG_Y else None
    gate_s = cx.dtmp("gate_s", [NCH * 128, 32 * 128], BF16)
    cmat_s = cx.dtmp("cmat_s", [NCH * 128, 8 * 128], BF16)
    ffn_s = cx.dtmp("ffn_s", [D, T], F32)
    sin_s = cx.dtmp("sin_s", [64, H * N], F32)

    cx.open_block()
    s = cx.s
    cc, _ = emit_common_consts(cx)

    banksF = [cx.ps(f"bF{i}", [128, 512], F32) for i in range(6)]
    banksB = [cx.ps(f"bB{i}", [128, 1024], BF16) for i in range(2)]

    cw = cx.sb("cw", [128, 32], F32)
    cb = cx.sb("cb", [128, 8], F32)
    nw_s = cx.sb("nw_s", [128, 32], F32)
    d_s = cx.sb("d_s", [128, 32], F32)
    bb_s = cx.sb("bb_s", [128, 32], F32)
    rep_s = cx.sb("rep_s", [8, 128], BF16)
    eye8_s = cx.sb("eye8_s", [8, 8], F32)
    m2t_s = cx.sb("m2t_s", [8, 8], F32)
    mask8_s = cx.sb("mask8_s", [8, 1], F32)
    slog_s = cx.sb("slog_s", [8, 64], F32)
    xn = cx.sb("xn_arena", [128, 32 * TE], BF16)
    yn = xn

    w = {}

    sld = cx.sem("sld")

    def f_sp0(sp):
        for dst, src in ((cw, d_convw), (cb, d_convb), (nw_s, d_nw), (d_s, d_dt),
                         (bb_s, d_bb), (rep_s, rep2), (eye8_s, eye8), (m2t_s, m2t),
                         (mask8_s, mask8), (slog_s, slogs)):
            sld.inc(sp.dma_start(out=dst[:, :], in_=src.ap()), 16)
        sld.inc(sp.dma_start(out=xn[:, :].rearrange("p (kk t) -> p kk t", kk=32),
                             in_=xn_i.ap().rearrange("(kk p) t -> p kk t", p=128)), 16)
        w["xn_ld"] = sld.n
    cx.sec("sp", f_sp0)

    cx.barrier()

    cx.push()
    coef = cx.sb("coef", [8, 64], F32)
    ccrow = cx.sb("ccrow", [1, 64], F32)
    cc_col = cx.sb("cc_col", [64, 1], F32)
    sj = [cx.sb(f"sj{i}", [64, H * N], F32) for i in range(2)]
    sacc2 = [cx.sb(f"sacc2{i}", [64, H * N], F32) for i in range(2)]

    def f_pe0(te):
        w["coef_mm"] = s["pe"].inc(
            te.matmul(banksF[0][0:8, 0:64], m2t_s[:, :], slog_s[:, :], start=True, stop=True))
    cx.sec("pe", f_pe0)

    def f_ac0(sc):
        sc.wait_ge(s["pe"].h, w["coef_mm"])
        w["coef_exp"] = s["ac"].inc(sc.activation(coef[:, :], banksF[0][0:8, 0:64], AF.Exp))
    cx.sec("ac", f_ac0)

    def f_ve0(ve):
        ve.wait_ge(s["ac"].h, w["coef_exp"])
        w["coef"] = s["ve"].inc(ve.tensor_scalar_mul(coef[:, :], coef[:, :], mask8_s[:, :]))
    cx.sec("ve", f_ve0)

    acc_at = [0] * 8
    for j in range(8):
        ldj = {}

        def f_sp(sp, j=j):
            if j >= 2:
                sp.wait_ge(s["ve"].h, acc_at[j - 2])
            ldj["x"] = cx.sem(f"sjld{j % 2}").inc(
                sp.dma_start(out=sj[j % 2][:, :], in_=sall.ap()[j * 64:(j + 1) * 64, :]), 16)
        cx.sec("sp", f_sp)

        def f_pe(te, j=j):
            te.wait_ge(s["ve"].h, w["coef"] if j == 0 else ldj.get("prev_ccev", acc_at[j - 1]))
            ldj["ccmm"] = s["pe"].inc(
                te.matmul(banksF[1][0:1, 0:64], eye8_s[:, j:j + 1], coef[:, :], start=True, stop=True))
        cx.sec("pe", f_pe)

        def f_ve(ve, j=j):
            ve.wait_ge(s["pe"].h, ldj["ccmm"])
            ldj["ccev"] = s["ve"].inc(ve.tensor_copy(ccrow[:, :], banksF[1][0:1, 0:64]))
        cx.sec("ve", f_ve)

        def f_pe2(te, j=j):
            te.wait_ge(s["ve"].h, ldj["ccev"])
            ldj["cctr"] = s["pe"].inc(
                te.transpose(banksF[2][0:64, 0:1], ccrow[:, :], cc["id_f32"][0:1, 0:1]))
        cx.sec("pe", f_pe2)

        def f_ve2(ve, j=j):
            ve.wait_ge(s["pe"].h, ldj["cctr"])
            v = s["ve"].inc(ve.tensor_copy(cc_col[:, :], banksF[2][0:64, 0:1]))
            ve.wait_ge(s["ve"].h, v)
            ve.wait_ge(cx.sem(f"sjld{j % 2}").h, ldj["x"])
            if j == 0:
                acc_at[0] = s["ve"].inc(
                    ve.tensor_scalar_mul(sacc2[1][:, :], sj[0][:, :], cc_col[:, :]))
            else:
                acc_at[j] = s["ve"].inc(
                    ve.scalar_tensor_tensor(sacc2[(j + 1) % 2][:, :], sj[j % 2][:, :],
                                            cc_col[:, :], sacc2[j % 2][:, :], ALU.mult, ALU.add))
        cx.sec("ve", f_ve2)

    def f_gp0(gp):
        gp.wait_ge(s["ve"].h, acc_at[7])
        w["sin_st"] = cx.sem("sinst").inc(gp.dma_start(out=sin_s.ap(), in_=sacc2[0][:, :]), 16)
    cx.sec("gp", f_gp0)

    cx.barrier()
    cx.pop()
    cx.push()

    wb = [cx.sb(f"wb{i}", [128, 4096], BF16) for i in range(3)]
    ext = [cx.sb(f"ext{i}", [128, TE], F32) for i in range(2)]
    cacc = [cx.sb(f"cacc{i}", [128, T], F32) for i in range(2)]
    obuf = [cx.sb(f"obuf{i}", [128, T], BF16) for i in range(2)]

    free_at = [None, None]
    casts = {}
    st_at = {}

    for ti in range(40):
        g = ti % 2
        r = emit_wtile_mm(cx, w_gc, ti, 128, wb, xn, banksF,
                          g, SPL, free_at[g], casts)

        if ti < 32:
            def f_ac(sc, ti=ti, g=g, r=r):
                sc.wait_ge(s["pe"].h, r["mm"])
                if ti >= 2:
                    sc.wait_ge(cx.sem(f"stp{g}").h, st_at[("dma", ti - 2)])
                ins = None
                for j, (a, b) in enumerate(SPL):
                    lo = max(a, 3)
                    ins = sc.activation(obuf[g][:, lo - 3:b - 3],
                                        banksF[g * 3 + j][:, lo - a:b - a], AF.Silu)
                st_at[("silu", ti)] = s["ac"].inc(ins)
                free_at[g] = (s["ac"], st_at[("silu", ti)])
            cx.sec("ac", f_ac)

            def f_gp(gp, ti=ti, g=g):
                gp.wait_ge(s["ac"].h, st_at[("silu", ti)])
                st_at[("dma", ti)] = cx.sem(f"stp{g}").inc(
                    gp.dma_start(out=gate_s.ap()[:, ti * 128:(ti + 1) * 128]
                                 .rearrange("(c p) t -> p c t", p=128),
                                 in_=obuf[g][:, :].rearrange("p (c t) -> p c t", c=NCH)), 16)
            cx.sec("gp", f_gp)
        else:
            def f_ve(ve, ti=ti, g=g, r=r):
                ve.wait_ge(s["pe"].h, r["mm"])
                if ti >= 2:
                    ve.wait_ge(s["ac"].h, st_at[("silu", ti - 2)])
                for j, (a, b) in enumerate(SPL):
                    ve.tensor_copy(ext[g][:, a:b], banksF[g * 3 + j][:, 0:b - a])
                v = s["ve"].inc(ve.engine_nop())
                free_at[g] = (s["ve"], v)
                ve.wait_ge(s["ve"].h, v)
                tt = ti - 32
                ve.tensor_scalar(cacc[g][:, :], ext[g][:, 0:T],
                                 cw[:, tt * 4:tt * 4 + 1], cb[:, tt:tt + 1], ALU.mult, ALU.add)
                for k in range(1, 4):
                    vv = s["ve"].inc(ve.engine_nop())
                    ve.wait_ge(s["ve"].h, vv)
                    ve.scalar_tensor_tensor(cacc[g][:, :], ext[g][:, k:k + T],
                                            cw[:, tt * 4 + k:tt * 4 + k + 1], cacc[g][:, :],
                                            ALU.mult, ALU.add)
                st_at[("conv", ti)] = s["ve"].inc(ve.engine_nop())
            cx.sec("ve", f_ve)

            def f_ac(sc, ti=ti, g=g):
                sc.wait_ge(s["ve"].h, st_at[("conv", ti)])
                if ti >= 2:
                    sc.wait_ge(cx.sem(f"stp{g}").h, st_at[("dma", ti - 2)])
                st_at[("silu", ti)] = s["ac"].inc(
                    sc.activation(obuf[g][:, :], cacc[g][:, :], AF.Silu))
            cx.sec("ac", f_ac)

            def f_gp(gp, ti=ti, g=g):
                gp.wait_ge(s["ac"].h, st_at[("silu", ti)])
                st_at[("dma", ti)] = cx.sem(f"stp{g}").inc(
                    gp.dma_start(out=cmat_s.ap()[:, (ti - 32) * 128:(ti - 31) * 128]
                                 .rearrange("(c p) t -> p c t", p=128),
                                 in_=obuf[g][:, :].rearrange("p (c t) -> p c t", c=NCH)), 16)
            cx.sec("gp", f_gp)

    gc_st_last = st_at[("dma", 39)]
    cx.barrier()

    logits = cx.sb("logits", [8, T], F32)
    ltm = cx.sb("ltm", [128, 8], F32)
    m1 = cx.sb("m1", [128, 1], F32)
    m2v = cx.sb("m2v", [128, 1], F32)
    eq1 = cx.sb("eq1", [128, 8], F32)
    eq2 = cx.sb("eq2", [128, 8], F32)
    lmask = cx.sb("lmask", [128, 8], F32)
    w1c = cx.sb("w1c", [128, 1], F32)
    w2c = cx.sb("w2c", [128, 1], F32)
    d21 = cx.sb("d21", [128, 1], F32)
    wtok = cx.sb("wtok", [128, 8], F32)
    wfm = cx.sb("wfm", [8, T], BF16)
    wrb = cx.sb("wrb", [128, 4096], BF16)
    h_sb = cx.sb("h_sb", [128, T], F32)
    hw = cx.sb("hw", [128, T], BF16)
    a2b = cx.sb("a2b", [128, 4096], BF16)
    bfl_b = cx.sb("bfl_b", [128, D], BF16)

    fld = cx.sem("fld")

    def f_sp1(sp):
        fld.inc(sp.dma_start(out=wrb[:, :], in_=w_r.ap()), 16)
        fld.inc(sp.dma_start(out=a2b[:, :], in_=d_a2t.ap()), 16)
        fld.inc(sp.dma_start(out=bfl_b[:, :], in_=d_bflat.ap()), 16)
        w["ffnw_ld"] = fld.n
    cx.sec("sp", f_sp1)

    def f_po1(gp):
        gp.wait_ge(fld.h, w["ffnw_ld"])
        w["ffnw_cast"] = s["po"].inc(gp.engine_nop())
    cx.sec("po", f_po1)

    def f_pe1(te):
        te.wait_ge(s["po"].h, w["ffnw_cast"])
        ins = None
        for half, (a, b) in enumerate(SPL2):
            for kk in range(32):
                ins = te.matmul(banksF[half][0:8, 0:b - a], wrb[:, kk * 128:kk * 128 + 8],
                                xn[:, kk * TE + a:kk * TE + b], start=(kk == 0), stop=(kk == 31))
        w["rt_mm"] = s["pe"].inc(ins)
    cx.sec("pe", f_pe1)

    def f_ve1(ve):
        ve.wait_ge(s["pe"].h, w["rt_mm"])
        for half, (a, b) in enumerate(SPL2):
            ve.tensor_copy(logits[:, a - 3:b - 3], banksF[half][0:8, 0:b - a])
        w["rt_ev"] = s["ve"].inc(ve.engine_nop())
    cx.sec("ve", f_ve1)

    def f_pe2(te):
        te.wait_ge(s["ve"].h, w["rt_ev"])
        ins = None
        for half, (a, b) in enumerate(SPL2):
            for kk in range(32):
                ins = te.matmul(banksF[4 + half][:, 0:b - a], a2b[:, kk * 128:(kk + 1) * 128],
                                xn[:, kk * TE + a:kk * TE + b], start=(kk == 0), stop=(kk == 31))
        w["h_mm"] = s["pe"].inc(ins)
    cx.sec("pe", f_pe2)

    def f_ve2(ve):
        ve.wait_ge(s["pe"].h, w["h_mm"])
        ve.tensor_copy(h_sb[:, 0:512], banksF[4][:, 0:512])
        w["h_ev"] = s["ve"].inc(ve.tensor_copy(h_sb[:, 512:1024], banksF[5][:, 0:512]))
    cx.sec("ve", f_ve2)

    wfm_ev = [0] * NCH
    for c in range(NCH):
        tk = {}

        def f_pe(te, c=c):
            if c == 0:
                te.wait_ge(s["ve"].h, w["rt_ev"])
            else:
                te.wait_ge(s["ve"].h, wfm_ev[c - 1])
            tk["ltm"] = s["pe"].inc(
                te.transpose(banksF[2][:, 0:8], logits[:, c * C:(c + 1) * C], cc["id_f32"][0:8, 0:8]))
        cx.sec("pe", f_pe)

        def f_ve(ve, c=c):
            ve.wait_ge(s["pe"].h, tk["ltm"])
            v = s["ve"].inc(ve.tensor_copy(ltm[:, :], banksF[2][:, 0:8]))
            ve.wait_ge(s["ve"].h, v)
            v = s["ve"].inc(ve.reduce_max(m1[:, :], ltm[:, :], axis=AXX))
            ve.wait_ge(s["ve"].h, v)
            v = s["ve"].inc(ve.tensor_scalar(eq1[:, :], ltm[:, :], m1[:, :], None, ALU.is_ge))
            ve.wait_ge(s["ve"].h, v)
            v = s["ve"].inc(ve.scalar_tensor_tensor(lmask[:, :], eq1[:, :], cc["cst"][:, 2:3],
                                                    ltm[:, :], ALU.mult, ALU.add))
            ve.wait_ge(s["ve"].h, v)
            v = s["ve"].inc(ve.reduce_max(m2v[:, :], lmask[:, :], axis=AXX))
            ve.wait_ge(s["ve"].h, v)
            tk["d21"] = s["ve"].inc(ve.tensor_sub(d21[:, :], m2v[:, :], m1[:, :]))
            tk["eq2"] = s["ve"].inc(ve.tensor_scalar(eq2[:, :], lmask[:, :], m2v[:, :], None, ALU.is_ge))
        cx.sec("ve", f_ve)

        def f_ac(sc, c=c):
            sc.wait_ge(s["ve"].h, tk["d21"])
            sc.activation(w2c[:, :], d21[:, :], AF.Sigmoid)
            tk["w12"] = s["ac"].inc(sc.activation(w1c[:, :], d21[:, :], AF.Sigmoid, scale=-1.0))
        cx.sec("ac", f_ac)

        def f_ve3(ve, c=c):
            ve.wait_ge(s["ac"].h, tk["w12"])
            ve.wait_ge(s["ve"].h, tk["eq2"])
            v = s["ve"].inc(ve.tensor_scalar_mul(wtok[:, :], eq1[:, :], w1c[:, :]))
            ve.wait_ge(s["ve"].h, v)
            tk["wtok"] = s["ve"].inc(
                ve.scalar_tensor_tensor(wtok[:, :], eq2[:, :], w2c[:, :], wtok[:, :],
                                        ALU.mult, ALU.add))
        cx.sec("ve", f_ve3)

        def f_pe2b(te, c=c):
            te.wait_ge(s["ve"].h, tk["wtok"])
            tk["wfm_mm"] = s["pe"].inc(
                te.transpose(banksF[3][0:8, 0:128], wtok[:, :], cc["id_f32"][:, :]))
        cx.sec("pe", f_pe2b)

        def f_ve4(ve, c=c):
            ve.wait_ge(s["pe"].h, tk["wfm_mm"])
            wfm_ev[c] = s["ve"].inc(ve.tensor_copy(wfm[:, c * C:(c + 1) * C], banksF[3][0:8, 0:128]))
        cx.sec("ve", f_ve4)

    def f_pe3(te):
        te.wait_ge(s["ve"].h, wfm_ev[NCH - 1])
        te.wait_ge(s["ve"].h, w["h_ev"])
        te.matmul(banksF[4][:, 0:512], rep_s[:, :], wfm[:, 0:512], start=True, stop=True)
        w["wbc_mm"] = s["pe"].inc(
            te.matmul(banksF[5][:, 0:512], rep_s[:, :], wfm[:, 512:1024], start=True, stop=True))
    cx.sec("pe", f_pe3)

    def f_ve5(ve):
        ve.wait_ge(s["pe"].h, w["wbc_mm"])
        ve.tensor_mul(hw[:, 0:512], h_sb[:, 0:512], banksF[4][:, 0:512])
        w["hw"] = s["ve"].inc(ve.tensor_mul(hw[:, 512:1024], h_sb[:, 512:1024], banksF[5][:, 0:512]))
    cx.sec("ve", f_ve5)

    cx.barrier()

    ffn_o = [cx.sb(f"ffn_o{i}", [128, T], F32) for i in range(2)]
    fev = {}
    fst = {}
    free_at = [None, None]
    wcast_prev = [None]
    r_prev_castv = [0]

    casts2 = {}
    mms2 = {}
    for ti in range(32):
        g = ti % 2
        b3 = ti % 3

        def mk_extra(te, ti=ti, g=g):
            ins = None
            for half, (a, b) in enumerate(SPL2):
                ins = te.matmul(banksF[g * 3 + half][:, 0:b - a],
                                bfl_b[:, ti * 128:(ti + 1) * 128],
                                hw[:, a - 3:b - 3], start=False, stop=True)
            return ins

        r = {}

        def f_sp(sp, ti=ti, g=g, b3=b3):
            if ti - 3 in mms2:
                sp.wait_ge(s["pe"].h, mms2[ti - 3])
            r["ld"] = cx.sem(f"wld{b3}").inc(
                sp.dma_start(out=wb[b3][:, :], in_=w_base.ap()[ti * 128:(ti + 1) * 128, :]), 16)
        cx.sec("sp", f_sp)

        def f_pe(te, ti=ti, g=g, b3=b3):
            te.wait_ge(cx.sem(f"wld{b3}").h, r["ld"])
            if free_at[g] is not None:
                sem, val = free_at[g]
                te.wait_ge(sem.h, val)
            ins = None
            for half, (a, b) in enumerate(SPL2):
                for kk in range(32):
                    te.matmul(banksF[g * 3 + half][:, 0:b - a], wb[b3][:, kk * 128:(kk + 1) * 128],
                              xn[:, kk * TE + a:kk * TE + b], start=(kk == 0), stop=False)
                ins = te.matmul(banksF[g * 3 + half][:, 0:b - a],
                                bfl_b[:, ti * 128:(ti + 1) * 128],
                                hw[:, a - 3:b - 3], start=False, stop=True)
            r["mm"] = s["pe"].inc(ins)
            mms2[ti] = r["mm"]
        cx.sec("pe", f_pe)

        def f_ve(ve, ti=ti, g=g):
            ve.wait_ge(s["pe"].h, r["mm"])
            if ti >= 2:
                ve.wait_ge(cx.sem(f"ffst{g}").h, fst[ti - 2])
            for half, (a, b) in enumerate(SPL2):
                ve.tensor_scalar_add(ffn_o[g][:, a - 3:b - 3],
                                     banksF[g * 3 + half][:, 0:b - a], bb_s[:, ti:ti + 1])
            fev[ti] = s["ve"].inc(ve.engine_nop())
            free_at[g] = (s["ve"], fev[ti])
        cx.sec("ve", f_ve)

        def f_gp(gp, ti=ti, g=g):
            gp.wait_ge(s["ve"].h, fev[ti])
            fst[ti] = cx.sem(f"ffst{g}").inc(
                gp.dma_start(out=ffn_s.ap()[ti * 128:(ti + 1) * 128, :], in_=ffn_o[g][:, :]), 16)
        cx.sec("gp", f_gp)

    cx.barrier()
    cx.pop()
    cx.push()

    HH = 32
    catm_s = cx.sb("catm_s", [128, 512], F32)
    atmf_s = cx.sb("atmf_s", [128, 512], F32)
    atmb_s = cx.sb("atmb_s", [128, 512], BF16)
    logcf_s = cx.sb("logcf_s", [128, 512], F32)
    elogcf_s = cx.sb("elogcf_s", [128, 512], F32)
    sinb = cx.sb("sinb", [128, H * P], BF16)
    ch_c = cx.sb("ch_c", [128, H * P], BF16)
    u_c = cx.sb("u_c", [128, H * P], BF16)
    xh_c = cx.sb("xh_c", [128, 32 * C], BF16)
    b_c = cx.sb("b_c", [128, 8 * C], BF16)
    c_c = cx.sb("c_c", [128, 8 * C], BF16)
    gate_c = cx.sb("gate_c", [128, 32 * C], BF16)
    y_fm = cx.sb("y_fm", [128, 32 * C], F32)
    dp_slab = cx.sb("dp_slab", [128, HH * C], F32)
    dm_slab = cx.sb("dm_slab", [128, HH * C], F32)
    cta_slab = cx.sb("cta_slab", [128, HH * C], BF16)
    ctb_slab = cx.sb("ctb_slab", [128, HH * C], BF16)
    mp_slab = cx.sb("mp_slab", [128, HH * C], BF16)
    cbt_sb = cx.sb("cbt_sb", [128, 4 * C], F32)
    ysq_s = cx.sb("ysq_s", [128, 32 * C], BF16)
    rst = cx.sb("rst", [1, 1024], F32)
    yt2 = cx.sb("yt2", [128, 128], F32)

    b2cl = cx.sem("b2cl")

    def f_sp2b(sp):
        sp.wait_ge(cx.sem("sinst").h, w["sin_st"])
        b2cl.inc(sp.dma_start(out=catm_s[:, :], in_=catm_i.ap()), 16)
        b2cl.inc(sp.dma_start(out=atmf_s[:, :], in_=atmf_i.ap()), 16)
        b2cl.inc(sp.dma_start(out=logcf_s[:, :], in_=logcf_i.ap()), 16)
        b2cl.inc(sp.dma_start(out=dp_slab[:, :].rearrange("p (h q) -> p h q", h=64),
                              in_=sin_s.ap().rearrange("h (p q) -> p h q", p=128)), 16)
        w["b2c_ld"] = b2cl.n
    cx.sec("sp", f_sp2b)

    def f_ve6(ve):
        ve.wait_ge(b2cl.h, w["b2c_ld"])
        ve.tensor_copy(sinb[:, :], dp_slab[:, :])
        w["sinb"] = s["ve"].inc(ve.tensor_copy(atmb_s[:, :], atmf_s[:, :]))
    cx.sec("ve", f_ve6)

    def f_ac6(sc):
        sc.wait_ge(b2cl.h, w["b2c_ld"])
        w["elogcf"] = s["ac"].inc(sc.activation(elogcf_s[:, :], logcf_s[:, :], AF.Exp))
    cx.sec("ac", f_ac6)

    yn_done = [0] * NCH
    prev_b2 = {}
    NU = 4
    UH = 16
    for c in range(NCH):
        bl = {}
        b2ld = cx.sem("b2ld")

        def f_sp(sp, c=c):
            if c >= 1:
                sp.wait_ge(s["ve"].h, yn_done[c - 1])
            b2ld.inc(sp.dma_start(out=ch_c[:, :], in_=ch_i.ap()[c * N:(c + 1) * N, :]), 16)
            b2ld.inc(sp.dma_start(out=u_c[:, :], in_=u_i.ap()[c * C:(c + 1) * C, :]), 16)
            b2ld.inc(sp.dma_start(out=xh_c[:, :], in_=xh_i.ap()[c * 128:(c + 1) * 128, :]), 16)
            b2ld.inc(sp.dma_start(out=b_c[:, :], in_=b_i.ap()[c * 128:(c + 1) * 128, :]), 16)
            if c == 0:
                sp.wait_ge(cx.sem("stp0").h, cx.sem("stp0").n)
                sp.wait_ge(cx.sem("stp1").h, cx.sem("stp1").n)
            b2ld.inc(sp.dma_start(out=c_c[:, :], in_=cmat_s.ap()[c * 128:(c + 1) * 128, :]), 16)
            b2ld.inc(sp.dma_start(out=gate_c[:, :], in_=gate_s.ap()[c * 128:(c + 1) * 128, :]), 16)
            bl["ld"] = b2ld.n
        cx.sec("sp", f_sp)

        dp_ev = {}
        cbt_ev = {}
        slab_rdy = {}
        y_mm = {}
        yev = {}

        def emit_A(u, c=c, bl=bl, dp_ev=dp_ev, cbt_ev=cbt_ev, slab_rdy=slab_rdy,
                   y_mm=y_mm, yev=yev):
            su = u % 2
            h0 = u * UH
            dps = dp_slab[:, su * UH * C:(su + 1) * UH * C]
            dms = dm_slab[:, su * UH * C:(su + 1) * UH * C]
            ctas = cta_slab[:, su * UH * C:(su + 1) * UH * C]
            ctbs = ctb_slab[:, su * UH * C:(su + 1) * UH * C]
            mps = mp_slab[:, su * UH * C:(su + 1) * UH * C]
            cbts = cbt_sb[:, su * 2 * C:(su + 1) * 2 * C]

            for q in range(UH // 4):
                def f_peQ(te, c=c, u=u, h0=h0, q=q):
                    if c == 0 and u == 0 and q == 0:
                        te.wait_ge(b2cl.h, w["b2c_ld"])
                        te.wait_ge(s["ve"].h, w["sinb"])
                    if q == 0 and u == 0:
                        te.wait_ge(b2ld.h, bl["ld"])
                        if c >= 1:
                            te.wait_ge(s["ve"].h, prev_b2[("norm_done", c - 1)])
                    if q >= 2:
                        te.wait_ge(s["ve"].h, dp_ev[(u, q - 2)])
                    else:
                        pk = (c, u - 1) if u >= 1 else ((c - 1, NU - 1) if c >= 1 else None)
                        if pk is not None:
                            te.wait_ge(s["ve"].h, prev_b2[("dp_done", pk)])
                    ins = None
                    for jj in range(4):
                        j = q * 4 + jj
                        h = h0 + j
                        ins = te.matmul(banksF[q % 2][:, jj * C:jj * C + C],
                                        atmb_s[:, c * 64 + h:c * 64 + h + 1].broadcast_to([128, 128]),
                                        cc["m_cum"][:, :], start=True, stop=True)
                    bl[f"dpmm{u}_{q}"] = s["pe"].inc(ins)
                cx.sec("pe", f_peQ)

                def f_veQ(ve, c=c, u=u, q=q, su=su, dps=dps):
                    if q == 0:
                        pk = (c, u - 2) if u >= 2 else ((c - 1, NU - 2 + u) if c >= 1 else None)
                        if pk is not None:
                            te_cnt = prev_b2.get(("y_done_pe", pk))
                            if te_cnt:
                                ve.wait_ge(s["pe"].h, te_cnt)
                        if DEBUG_Y:
                            ve.wait_ge(cx.sem("dbgst").h, cx.sem("dbgst").n)
                    ve.wait_ge(s["pe"].h, bl[f"dpmm{u}_{q}"])
                    dp_ev[(u, q)] = s["ve"].inc(
                        ve.tensor_copy(dps[:, q * 512:(q + 1) * 512], banksF[q % 2][:, 0:512]))
                cx.sec("ve", f_veQ)
            prev_b2[("dp_done", (c, u))] = dp_ev[(u, UH // 4 - 1)]

            for gq in range(2):
                def f_peC(te, c=c, u=u, gq=gq):
                    g = u * 2 + gq
                    pk = (c, u - 1) if u >= 1 else ((c - 1, NU - 1) if c >= 1 else None)
                    if pk is not None and gq == 0:
                        te.wait_ge(s["ve"].h, prev_b2[("cbt_done", pk)])
                    bl[f"cbt{u}_{gq}"] = s["pe"].inc(
                        te.matmul(banksF[2 + gq][:, 0:128], b_c[:, g * 128:(g + 1) * 128],
                                  c_c[:, g * 128:(g + 1) * 128], start=True, stop=True))
                cx.sec("pe", f_peC)

                def f_veCb(ve, c=c, u=u, gq=gq, cbts=cbts):
                    ve.wait_ge(s["pe"].h, bl[f"cbt{u}_{gq}"])
                    cbt_ev[(u, gq)] = s["ve"].inc(
                        ve.tensor_mul(cbts[:, gq * C:(gq + 1) * C],
                                      banksF[2 + gq][:, 0:128], cc["ut01f"][:, :]))
                cx.sec("ve", f_veCb)
            prev_b2[("cbt_done", (c, u))] = cbt_ev[(u, 1)]

            def f_veDm(ve, c=c, u=u, h0=h0, dps=dps, dms=dms):
                ve.wait_ge(s["ve"].h, dp_ev[(u, UH // 4 - 1)])
                catm_rep = catm_s[:, c * 64 + h0:c * 64 + h0 + UH].unsqueeze(2).broadcast_to([128, UH, 128])
                v = s["ve"].inc(
                    ve.tensor_tensor(dms[:, :].rearrange("p (j t) -> p j t", j=UH),
                                     dps[:, :].rearrange("p (j t) -> p j t", j=UH),
                                     catm_rep, ALU.subtract))
                ve.wait_ge(s["ve"].h, v)
                bl[f"dm{u}"] = s["ve"].inc(
                    ve.tensor_single_scalar(dms[:, :], dms[:, :], 0.0, ALU.min))
            cx.sec("ve", f_veDm)

            def f_acA(sc, c=c, u=u, dps=dps, dms=dms):
                if c == 0 and u == 0:
                    sc.wait_ge(s["ac"].h, w["elogcf"])
                sc.wait_ge(s["ve"].h, bl[f"dm{u}"])
                sc.activation(dps[:, :], dps[:, :], AF.Exp)
                bl[f"exp{u}"] = s["ac"].inc(sc.activation(dms[:, :], dms[:, :], AF.Exp))
            cx.sec("ac", f_acA)

            def f_veB(ve, c=c, u=u, h0=h0, dps=dps, dms=dms, ctas=ctas, ctbs=ctbs,
                      mps=mps, cbts=cbts):
                ve.wait_ge(s["ac"].h, bl[f"exp{u}"])
                ve.wait_ge(s["ve"].h, cbt_ev[(u, 1)])
                c_rep = c_c[:, u * 2 * 128:(u * 2 + 2) * 128].rearrange(
                    "p (g t) -> p g t", g=2).unsqueeze(2).broadcast_to([128, 2, 8, 128])
                v1 = s["ve"].inc(
                    ve.tensor_tensor(ctas[:, :].rearrange("p (g j t) -> p g j t", g=2, j=8),
                                     dps[:, :].rearrange("p (g j t) -> p g j t", g=2, j=8),
                                     c_rep, ALU.mult))
                elog_rep = elogcf_s[:, c * 64 + h0:c * 64 + h0 + UH].unsqueeze(2).broadcast_to([128, UH, 128])
                ve.wait_ge(s["ve"].h, v1)
                v2 = s["ve"].inc(
                    ve.tensor_tensor(ctbs[:, :].rearrange("p (j t) -> p j t", j=UH),
                                     ctas[:, :].rearrange("p (j t) -> p j t", j=UH),
                                     elog_rep, ALU.mult))
                cbt_rep = cbts[:, :].rearrange("p (g t) -> p g t", g=2).unsqueeze(2).broadcast_to([128, 2, 8, 128])
                v3 = s["ve"].inc(
                    ve.tensor_tensor(mps[:, :].rearrange("p (g j t) -> p g j t", g=2, j=8),
                                     dms[:, :].rearrange("p (g j t) -> p g j t", g=2, j=8),
                                     cbt_rep, ALU.mult))
                slab_rdy[u] = v3
            cx.sec("ve", f_veB)

        def emit_Y(u, c=c, bl=bl, slab_rdy=slab_rdy, y_mm=y_mm, yev=yev):
            su = u % 2
            h0 = u * UH
            ctas = cta_slab[:, su * UH * C:(su + 1) * UH * C]
            ctbs = ctb_slab[:, su * UH * C:(su + 1) * UH * C]
            mps = mp_slab[:, su * UH * C:(su + 1) * UH * C]
            for pi in range(UH // 2):
                def f_peP(te, c=c, u=u, h0=h0, pi=pi, ctas=ctas, ctbs=ctbs, mps=mps):
                    if pi == 0:
                        te.wait_ge(s["ve"].h, slab_rdy[u])
                    bank = 4 + pi % 2
                    if pi >= 2:
                        te.wait_ge(s["ve"].h, yev[(u, pi - 2)])
                    else:
                        pk = (c, u - 1) if u >= 1 else ((c - 1, NU - 1) if c >= 1 else None)
                        if pk is not None:
                            te.wait_ge(s["ve"].h, prev_b2[("y_done", pk)])
                    h = h0 + 2 * pi
                    j = 2 * pi
                    ins = None
                    for sub in range(2):
                        hq = h + sub
                        jq = j + sub
                        pslice = banksF[bank][sub * 64:sub * 64 + 64, 0:128]
                        te.matmul(pslice, u_c[:, hq * P:(hq + 1) * P],
                                  mps[:, jq * 128:(jq + 1) * 128], start=True, stop=False)
                        te.matmul(pslice, ch_c[:, hq * P:(hq + 1) * P],
                                  ctas[:, jq * 128:(jq + 1) * 128], start=False, stop=False)
                        ins = te.matmul(pslice, sinb[:, hq * P:(hq + 1) * P],
                                        ctbs[:, jq * 128:(jq + 1) * 128], start=False, stop=True)
                    y_mm[(u, pi)] = s["pe"].inc(ins)
                cx.sec("pe", f_peP)

                def f_veP(ve, c=c, u=u, pi=pi):
                    bank = 4 + pi % 2
                    pair = u * (UH // 2) + pi
                    ve.wait_ge(s["pe"].h, y_mm[(u, pi)])
                    yev[(u, pi)] = s["ve"].inc(
                        ve.scalar_tensor_tensor(
                            y_fm[:, pair * C:(pair + 1) * C],
                            xh_c[:, pair * C:(pair + 1) * C],
                            d_s[:, pair:pair + 1],
                            banksF[bank][:, 0:128], ALU.mult, ALU.add))
                cx.sec("ve", f_veP)
            prev_b2[("y_done", (c, u))] = yev[(u, UH // 2 - 1)]
            prev_b2[("y_done_pe", (c, u))] = y_mm[(u, UH // 2 - 1)]

        emit_A(0)
        emit_A(1)
        emit_Y(0)
        emit_A(2)
        emit_Y(1)
        emit_A(3)
        emit_Y(2)
        emit_Y(3)

        if DEBUG_Y:
            def f_gpD(gp, c=c):
                gp.wait_ge(s["ve"].h, yev[(NU - 1, UH // 2 - 1)])
                cx.sem("dbgst").inc(
                    gp.dma_start(out=dbg_y.ap()[c * 128:(c + 1) * 128, :], in_=y_fm[:, :]), 16)
            cx.sec("gp", f_gpD)

        def f_acN(sc, c=c):
            sc.wait_ge(s["ve"].h, yev[(NU - 1, UH // 2 - 1)])
            bl["ysq"] = s["ac"].inc(sc.activation(ysq_s[:, :], y_fm[:, :], AF.Square))
        cx.sec("ac", f_acN)

        gn_mm = [0] * 32

        def f_peN(te, c=c):
            te.wait_ge(s["ac"].h, bl["ysq"])
            te.wait_ge(s["ve"].h, prev_b2[("dp_done", (c, NU - 1))])
            te.wait_ge(s["ve"].h, prev_b2[("cbt_done", (c, NU - 1))])
            for t in range(32):
                gq = t // 4
                bank = gq // 4
                gn_mm[t] = s["pe"].inc(
                    te.matmul(banksF[bank][0:1, (gq % 4) * C:(gq % 4) * C + C],
                              cc["ones_bf"][:, :], ysq_s[:, t * C:(t + 1) * C],
                              start=(t % 4 == 0), stop=(t % 4 == 3)))
        cx.sec("pe", f_peN)

        def f_acN2(sc, c=c):
            sc.wait_ge(s["pe"].h, gn_mm[31])
            sc.activation(rst[0:1, 0:512], banksF[0][0:1, 0:512], AF.Sqrt,
                          bias=cc["cst"][0:1, 0:1], scale=1.0 / 512)
            bl["gn_sq"] = s["ac"].inc(
                sc.activation(rst[0:1, 512:1024], banksF[1][0:1, 0:512], AF.Sqrt,
                              bias=cc["cst"][0:1, 0:1], scale=1.0 / 512))
        cx.sec("ac", f_acN2)

        def f_veN(ve, c=c):
            ve.wait_ge(s["ac"].h, bl["gn_sq"])
            bl["gn_rc"] = s["ve"].inc(ve.reciprocal(rst[0:1, :], rst[0:1, :]))
        cx.sec("ve", f_veN)

        gn_ap = [0] * 8
        for gq in range(8):
            def f_peG(te, c=c, gq=gq):
                te.wait_ge(s["ve"].h, bl["gn_rc"])
                if gq >= 2:
                    te.wait_ge(s["ve"].h, gn_ap[gq - 2])
                bl[f"gn_bc{gq}"] = s["pe"].inc(
                    te.matmul(banksF[2 + gq % 2][:, 0:128], cc["ones_row"][:, :],
                              rst[0:1, gq * 128:(gq + 1) * 128], start=True, stop=True))
            cx.sec("pe", f_peG)

            def f_veG(ve, c=c, gq=gq):
                ve.wait_ge(s["pe"].h, bl[f"gn_bc{gq}"])
                v = gn_ap[gq - 1] if gq >= 1 else 0
                for tt in range(4):
                    t = gq * 4 + tt
                    if v:
                        ve.wait_ge(s["ve"].h, v)
                    v = s["ve"].inc(
                        ve.scalar_tensor_tensor(yt2[:, :], y_fm[:, t * C:(t + 1) * C],
                                                nw_s[:, t:t + 1], banksF[2 + gq % 2][:, 0:128],
                                                ALU.mult, ALU.mult))
                    ve.wait_ge(s["ve"].h, v)
                    v = s["ve"].inc(ve.tensor_mul(yn[:, t * TE + c * C:t * TE + (c + 1) * C],
                                                  yt2[:, :], gate_c[:, t * C:(t + 1) * C]))
                gn_ap[gq] = v
            cx.sec("ve", f_veG)

        yn_done[c] = gn_ap[7]
        prev_b2[("norm_done", c)] = gn_ap[7]

    cx.barrier()
    cx.pop()
    cx.push()

    wb = [cx.sb(f"wbo{i}", [128, 4096], BF16) for i in range(3)]
    x_d = [cx.sb(f"x_d{i}", [128, T], F32) for i in range(2)]
    f_d = [cx.sb(f"f_d{i}", [128, T], F32) for i in range(2)]
    o_d = [cx.sb(f"o_d{i}", [128, T], F32) for i in range(2)]
    oev = {}
    ost = {}
    free_at = [None, None]
    casts3 = {}
    mms3 = {}

    for ti in range(32):
        g = ti % 2
        b3 = ti % 3
        ro = {}

        def f_sp(sp, ti=ti, g=g, b3=b3):
            if ti - 3 in mms3:
                sp.wait_ge(s["pe"].h, mms3[ti - 3])
            if ti >= 2:
                sp.wait_ge(s["ve"].h, oev[ti - 2])
            if ti < 2:
                sp.wait_ge(cx.sem("ffst0").h, cx.sem("ffst0").n)
                sp.wait_ge(cx.sem("ffst1").h, cx.sem("ffst1").n)
            osem = cx.sem(f"old{b3}")
            osem.inc(sp.dma_start(out=wb[b3][:, :], in_=w_out.ap()[ti * 128:(ti + 1) * 128, :]), 16)
            osem.inc(sp.dma_start(out=x_d[g][:, :], in_=xT.ap()[ti * 128:(ti + 1) * 128, 3:TE]), 16)
            osem.inc(sp.dma_start(out=f_d[g][:, :], in_=ffn_s.ap()[ti * 128:(ti + 1) * 128, :]), 16)
            ro["ld"] = osem.n
        cx.sec("sp", f_sp)

        def f_pe(te, ti=ti, g=g, b3=b3):
            te.wait_ge(cx.sem(f"old{b3}").h, ro["ld"])
            if free_at[g] is not None:
                sem, val = free_at[g]
                te.wait_ge(sem.h, val)
            ins = None
            for half in range(2):
                for kk in range(32):
                    ins = te.matmul(banksF[g * 3 + half][:, 0:512],
                                    wb[b3][:, kk * 128:(kk + 1) * 128],
                                    yn[:, kk * TE + half * 512:kk * TE + half * 512 + 512],
                                    start=(kk == 0), stop=(kk == 31))
            ro["mm"] = s["pe"].inc(ins)
            mms3[ti] = ro["mm"]
        cx.sec("pe", f_pe)

        def f_ve(ve, ti=ti, g=g):
            ve.wait_ge(s["pe"].h, ro["mm"])
            if ti >= 2:
                ve.wait_ge(cx.sem(f"ostp{g}").h, ost[ti - 2])
            for half in range(2):
                ve.tensor_add(o_d[g][:, half * 512:(half + 1) * 512],
                              banksF[g * 3 + half][:, 0:512],
                              x_d[g][:, half * 512:(half + 1) * 512])
            v = s["ve"].inc(ve.engine_nop())
            free_at[g] = (s["ve"], v)
            ve.wait_ge(s["ve"].h, v)
            oev[ti] = s["ve"].inc(ve.tensor_add(o_d[g][:, :], o_d[g][:, :], f_d[g][:, :]))
        cx.sec("ve", f_ve)

        def f_gp(gp, ti=ti, g=g):
            gp.wait_ge(s["ve"].h, oev[ti])
            ost[ti] = cx.sem(f"ostp{g}").inc(
                gp.dma_start(out=out_e.ap()[ti * 128:(ti + 1) * 128, :], in_=o_d[g][:, :]), 16)
        cx.sec("gp", f_gp)

    def f_fin(gp):
        for nm in ("ostp0", "ostp1"):
            gp.wait_ge(cx.sem(nm).h, cx.sem(nm).n)
    cx.sec("gp", f_fin)

    cx.close_block()
    while len(cx.stacks) > 1:
        cx.pop()
    cx.st.close()
    return cx.nc


def tile_w(W):
    dcol = W.shape[1]
    nt = (dcol + 127) // 128
    Wp = np.zeros((D, nt * 128), np.float32)
    Wp[:, :dcol] = W
    return np.ascontiguousarray(
        Wp.reshape(32, 128, nt, 128).transpose(2, 1, 0, 3)
        .reshape(nt * 128, 4096).astype(BF))


def _host_inputs(args):
    (x, w_in, convw, convb, dtb, alog, Dv, nw, w_outm, w_basem, b_base, w_rm,
     Am, Bm) = args
    f32 = np.float32
    xt_full = np.ascontiguousarray(x.reshape(L, D).T)
    w_in_t = np.ascontiguousarray(w_in.T)
    host = {}
    host["w_pre"] = tile_w(
        np.concatenate([w_in_t[:, 4096:9216], w_in_t[:, 10240:10304]], axis=1))
    host["w_gc"] = tile_w(
        np.concatenate([w_in_t[:, 0:4096], w_in_t[:, 9216:10240]], axis=1))
    host["w_base"] = tile_w(w_basem.T)
    host["w_out"] = tile_w(w_outm.T)
    host["w_r"] = tile_w(w_rm.T)
    host["a2t"] = tile_w(Am.reshape(128, D).T)
    host["bflat"] = np.ascontiguousarray(
        Bm.transpose(0, 2, 1).reshape(128, D).astype(BF))
    host["convw"] = np.ascontiguousarray(
        convw[0:5120].reshape(40, 128, 4).transpose(1, 0, 2).reshape(128, 160))
    host["convb"] = np.ascontiguousarray(convb[0:5120].reshape(40, 128).T)
    host["convw_c"] = np.ascontiguousarray(
        convw[5120:6144].reshape(8, 128, 4).transpose(1, 0, 2).reshape(128, 32))
    host["convb_c"] = np.ascontiguousarray(convb[5120:6144].reshape(8, 128).T)
    host["nw_t"] = np.ascontiguousarray(nw.reshape(32, 128).T)
    host["d_t"] = np.ascontiguousarray(np.repeat(Dv, P).reshape(32, 128).T)
    host["bb_t"] = np.ascontiguousarray(b_base.reshape(32, 128).T)
    host["dt_bias"] = np.ascontiguousarray(dtb.reshape(64, 1))
    host["a_log"] = np.ascontiguousarray(alog.reshape(64, 1))
    host["eye8"] = np.eye(8, dtype=f32)
    rep2 = np.zeros((8, 128), dtype=BF)
    for e in range(8):
        rep2[e, e * 16:(e + 1) * 16] = BF(2.0)
    host["rep2"] = rep2
    host["x_slices"] = []
    for i in range(NCORES):
        sl = np.zeros((D, TE), f32)
        lo = i * T - 3
        if lo < 0:
            sl[:, 3:] = xt_full[:, 0:T]
        else:
            sl[:, :] = xt_full[:, lo:lo + TE]
        host["x_slices"].append(np.ascontiguousarray(sl))
    return host


def kernel(L0_attn_hidden_states, L0_attn_in_proj_w, L0_attn_conv_weight,
           L0_attn_conv_bias, L0_attn_dt_bias, L0_attn_A_log, L0_attn_D,
           L0_attn_norm_weight, L0_attn_out_proj_w, L0_ffn_W_base,
           L0_ffn_b_base, L0_ffn_W_router, L0_ffn_A, L0_ffn_B):
    f32 = np.float32
    args = [np.asarray(a, f32) for a in (
        L0_attn_hidden_states, L0_attn_in_proj_w, L0_attn_conv_weight,
        L0_attn_conv_bias, L0_attn_dt_bias, L0_attn_A_log, L0_attn_D,
        L0_attn_norm_weight, L0_attn_out_proj_w, L0_ffn_W_base,
        L0_ffn_b_base, L0_ffn_W_router, L0_ffn_A, L0_ffn_B)]
    host = _host_inputs([args[0], args[1], args[2], args[3], args[4], args[5],
                         args[6], args[7], args[8], args[9], args[10], args[11],
                         args[12], args[13]])
    cnp = np_consts()

    nc1 = build_pre()
    in1 = []
    for i in range(NCORES):
        m = dict(xT=host["x_slices"][i], w_pre=host["w_pre"], convw=host["convw"],
                 convb=host["convb"], dt_bias=host["dt_bias"], a_log=host["a_log"],
                 id_bf=cnp["id_bf"], id_f32=cnp["id_f32"], m_cum=cnp["m_cum"],
                 maskb=cnp["maskb"], ut01f=cnp["ut01f"])
        in1.append(m)
    r1 = run_bass_kernel_spmd(nc1, in1, core_ids=list(range(NCORES))).results

    sall = np.ascontiguousarray(np.stack(
        [np.asarray(r1[i]["sl_e"]).reshape(128, 64, 64).transpose(1, 0, 2).reshape(64, H * N)
         for i in range(NCORES)], axis=0).reshape(NCORES * 64, H * N))
    slogs = np.ascontiguousarray(np.concatenate(
        [np.asarray(r1[i]["slog_e"])[:, 0:64] for i in range(NCORES)], axis=0))

    nc2 = build_main()
    in2 = []
    for i in range(NCORES):
        m2 = np.zeros((8, 8), f32)
        for k in range(8):
            for j in range(8):
                if j < k < i:
                    m2[k, j] = 1.0
        mask8 = (np.arange(8) < i).astype(f32).reshape(8, 1)
        m = dict(xT=host["x_slices"][i], xn_i=np.asarray(r1[i]["xn_e"]),
                 w_gc=host["w_gc"], convw_c=host["convw_c"], convb_c=host["convb_c"],
                 w_base=host["w_base"], bb_t=host["bb_t"], w_r=host["w_r"],
                 a2t=host["a2t"], bflat=host["bflat"], w_out=host["w_out"],
                 nw_t=host["nw_t"], d_t=host["d_t"],
                 xh_i=np.asarray(r1[i]["xh_e"]), b_i=np.asarray(r1[i]["b_e"]),
                 u_i=np.asarray(r1[i]["u_e"]), ch_i=np.asarray(r1[i]["ch_e"]),
                 catm_i=np.asarray(r1[i]["catm_e"]), atmf_i=np.asarray(r1[i]["atmf_e"]),
                 logcf_i=np.asarray(r1[i]["logcf_e"]),
                 sall=sall, slogs=slogs, m2t=np.ascontiguousarray(m2), mask8=mask8,
                 eye8=host["eye8"], rep2=host["rep2"],
                 id_bf=cnp["id_bf"], id_f32=cnp["id_f32"], m_cum=cnp["m_cum"],
                 maskb=cnp["maskb"], ut01f=cnp["ut01f"])
        in2.append(m)
    r2 = run_bass_kernel_spmd(nc2, in2, core_ids=list(range(NCORES))).results

    out = np.concatenate([np.asarray(r2[i]["out_e"]).T for i in range(NCORES)], axis=0)
    return np.ascontiguousarray(out.reshape(1, L, D)).astype(np.float32)
